# version 01
# baseline (speedup 1.0000x reference)
import numpy as np
from contextlib import ExitStack

import concourse.bass as bass
import concourse.mybir as mybir
from concourse.bass_utils import run_bass_kernel_spmd

F32 = mybir.dt.float32
BF16 = mybir.dt.bfloat16
AF = mybir.ActivationFunctionType
ALU = mybir.AluOpType
AX = mybir.AxisListType

P = 128
T = 2048
NT = 16
C = 4096
NCC = 32
DV = 1024
NDC = 8
NFC = 24
SEQ = 16384
NBLK = 8
NKC = 16
NH = 2
EPS = 1e-5
QSCALE = 1.0 / 32.0


def build_nc(upto=5, taps=()):
    nc = bass.Bass("TRN2", target_bir_lowering=False)

    x_hbm = nc.dram_tensor("x", [T, C], F32, kind="ExternalInput").ap()
    wqkv_b = nc.dram_tensor("wqkv", [NFC, P, NCC, P], F32, kind="ExternalInput").ap()
    wo_b = nc.dram_tensor("wo", [NCC, P, NDC, P], F32, kind="ExternalInput").ap()
    wt_b = nc.dram_tensor("wt", [NCC, P, NCC, P], F32, kind="ExternalInput").ap()
    bqkv_h = nc.dram_tensor("bqkv", [1, 3072], F32, kind="ExternalInput").ap()
    bo_h = nc.dram_tensor("bo", [1, C], F32, kind="ExternalInput").ap()
    bffn_h = nc.dram_tensor("bffn", [1, C], F32, kind="ExternalInput").ap()
    a_h = nc.dram_tensor("a_nat", [32, C], F32, kind="ExternalInput").ap()
    wr_h = nc.dram_tensor("wr_nat", [4, C], F32, kind="ExternalInput").ap()
    bm_h = nc.dram_tensor("bmf", [32, C], F32, kind="ExternalInput").ap()
    y_hbm = nc.dram_tensor("y", [T, C], F32, kind="ExternalOutput").ap()

    kt_stage = nc.dram_tensor("kt_stage", [DV, T], BF16).ap()
    vt_dram = nc.dram_tensor("vt_dram", [DV, T], BF16).ap()
    v_stage = nc.dram_tensor("v_stage", [T, DV], BF16).ap()
    gk = nc.dram_tensor("gk", [8 * DV, T], BF16).ap()
    gv = nc.dram_tensor("gv", [SEQ, DV], BF16).ap()
    ffn_dram = nc.dram_tensor("ffn_dram", [C, T], F32).ap()

    tap_out = {n: nc.dram_tensor(n, sh, dt, kind="ExternalOutput").ap()
               for n, sh, dt in taps}

    ident_bf = nc.alloc_sbuf_tensor("ident_bf", [P, P], BF16)
    ident_f = nc.alloc_sbuf_tensor("ident_f", [P, P], F32)
    ones8 = nc.alloc_sbuf_tensor("ones8", [P, 8], BF16)
    bq_col = nc.alloc_sbuf_tensor("bq_col", [P, NFC], F32)
    bq_q = nc.alloc_sbuf_tensor("bq_q", [P, 8], F32)
    bo_col = nc.alloc_sbuf_tensor("bo_col", [P, NCC], F32)
    bf_col = nc.alloc_sbuf_tensor("bf_col", [P, NCC], F32)
    sumc = nc.alloc_sbuf_tensor("sumc", [P, NT], F32)
    sqc = nc.alloc_sbuf_tensor("sqc", [P, NT], F32)
    meanc = nc.alloc_sbuf_tensor("meanc", [P, NT], F32)
    rstdc = nc.alloc_sbuf_tensor("rstdc", [P, NT], F32)
    nbc = nc.alloc_sbuf_tensor("nbc", [P, NT], F32)
    t0c = nc.alloc_sbuf_tensor("t0c", [P, 6], F32)
    sab = nc.alloc_sbuf_tensor("sab", [P, 4 * NT], F32)
    s_sb = nc.alloc_sbuf_tensor("s_sb", [P, NT], F32)
    rs_sb = nc.alloc_sbuf_tensor("rs_sb", [P, NT], F32)

    st = ExitStack()
    with st:
        E = st.enter_context
        names = ("sgp sld sbt sx0 sx1 ssd sxn stp stpe "
                 "swq0 swq1 sqk sqe skst0 skst1 svtl svst0 svst1 scc "
                 "sg1 sg3 swt0 swt1 sfm sfe sfd0 sfd1 "
                 "skl svl sstm sexp spv sov sot sote "
                 "swo0 swo1 sfr0 sfr1 sat sy1 syt sx20 sx21 syv "
                 "syo0 syo1 vself sself stap").split()
        S = {n: E(nc.semaphore(n)) for n in names}
        vn = [0]
        sn = [0]

        blk = E(nc.Block())

        def g_init(g):
            g.memset(ident_bf[:, :], 0.0).then_inc(S["sgp"], 1)
            g.wait_ge(S["sgp"], 1)
            g.affine_select(out=ident_bf[:, :], in_=ident_bf[:, :],
                            compare_op=ALU.not_equal, fill=1.0, base=0,
                            pattern=[[-1, P]], channel_multiplier=1).then_inc(S["sgp"], 1)
            g.memset(ident_f[:, :], 0.0).then_inc(S["sgp"], 1)
            g.wait_ge(S["sgp"], 3)
            g.affine_select(out=ident_f[:, :], in_=ident_f[:, :],
                            compare_op=ALU.not_equal, fill=1.0, base=0,
                            pattern=[[-1, P]], channel_multiplier=1).then_inc(S["sgp"], 1)
            g.memset(ones8[:, :], 1.0).then_inc(S["sgp"], 1)
        blk.gpsimd(g_init)

        with nc.sbuf_tensor("bq_row", [1, 3072], F32) as bq_row, \
             nc.sbuf_tensor("bo_row", [1, C], F32) as bo_row, \
             nc.sbuf_tensor("bf_row", [1, C], F32) as bf_row, \
             nc.psum_tensor("bias_ps", [P, 128], F32) as bias_ps:

            def g_rows(g):
                g.dma_start(out=bq_row[:, :], in_=bqkv_h).then_inc(S["sld"], 16)
                g.wait_ge(S["sld"], 16)
                g.dma_start(out=bo_row[:, :], in_=bo_h).then_inc(S["sld"], 16)
                g.wait_ge(S["sld"], 32)
                g.dma_start(out=bf_row[:, :], in_=bffn_h).then_inc(S["sld"], 16)
            blk.gpsimd(g_rows)

            def t_bias(t):
                t.wait_ge(S["sgp"], 5)
                t.wait_ge(S["sld"], 48)
                k = 0
                for row, cnt in ((bq_row, NFC), (bo_row, NCC), (bf_row, NCC)):
                    for j in range(cnt):
                        mm = t.transpose(bias_ps[:, k:k + 1],
                                         row[0:1, j * P:(j + 1) * P],
                                         ident_f[0:1, 0:1])
                        k += 1
                mm.then_inc(S["sbt"], 1)
            blk.tensor(t_bias)

            def v_bias(v):
                v.wait_ge(S["sbt"], 1)
                v.tensor_copy(bq_col[:, :], bias_ps[:, 0:NFC]).then_inc(S["vself"], 1)
                vn[0] += 1
                v.tensor_copy(bo_col[:, :], bias_ps[:, NFC:NFC + NCC])
                v.tensor_copy(bf_col[:, :], bias_ps[:, NFC + NCC:NFC + 2 * NCC])
                v.wait_ge(S["vself"], vn[0])
                v.tensor_scalar(bq_q[:, :], bq_col[:, 0:8], QSCALE, None,
                                ALU.mult).then_inc(S["vself"], 1)
                vn[0] += 1
            blk.vector(v_bias)
        VBIAS_DONE = vn[0]

        QT_cm = nc.sbuf_tensor("QT", [P, NDC * T], BF16)
        QT = QT_cm.__enter__()
        xnT_cm = nc.sbuf_tensor("xnT", [P, NCC * T], BF16)
        xnT = xnT_cm.__enter__()

        _phase0(nc, blk, S, vn, sn, x_hbm, xnT, ident_bf,
                sumc, sqc, meanc, rstdc, nbc, t0c, sab)

        if upto >= 2:
            _phase2(nc, blk, S, xnT, QT, wqkv_b, kt_stage, vt_dram, v_stage,
                    gk, gv, bq_col, bq_q)
        if upto >= 3:
            m_hT = _phase3(nc, blk, S, vn, xnT, wt_b, a_h, wr_h, bm_h,
                           bf_col, ident_bf, ident_f, ffn_dram)
        xnT_cm.__exit__(None, None, None)

        OT_cm = nc.sbuf_tensor("OT", [P, NDC * T], BF16)
        OT = OT_cm.__enter__()
        if upto >= 4:
            _phase4(nc, blk, S, vn, QT, OT, gk, gv, ones8, ident_bf, s_sb, rs_sb)
        if upto >= 5:
            _phase5(nc, blk, S, OT, wo_b, ffn_dram, x_hbm, y_hbm, bo_col, ident_f)

        _emit_taps(nc, blk, S, vn, tap_out, upto,
                   dict(xnT=xnT if upto < 4 else None, QT=QT, OT=OT,
                        gk=gk, gv=gv, ffn_dram=ffn_dram,
                        s_sb=s_sb, rs_sb=rs_sb))
        OT_cm.__exit__(None, None, None)
        QT_cm.__exit__(None, None, None)

    return nc, tap_out


def _phase0(nc, blk, S, vn, sn, x_hbm, xnT, ident_bf,
            sumc, sqc, meanc, rstdc, nbc, t0c, sab):
    with nc.sbuf_tensor("x_t0", [P, C], F32) as x_t0, \
         nc.sbuf_tensor("x_t1", [P, C], F32) as x_t1, \
         nc.sbuf_tensor("xn_t", [P, C], BF16) as xn_t, \
         nc.psum_tensor("trash_ps", [P, T], F32) as trash_ps, \
         nc.psum_tensor("tp_ps0", [P, 1024], BF16) as tp_ps0, \
         nc.psum_tensor("tp_ps1", [P, 1024], BF16) as tp_ps1:
        x_t = (x_t0, x_t1)
        tp_ps = (tp_ps0, tp_ps1)

        def sy_p0(sy):
            sy.wait_ge(S["sbt"], 1)
            for i in range(NT):
                if i >= 2:
                    sy.wait_ge(S["sxn"], i - 1)
                sy.dma_start(out=x_t[i % 2][:, :],
                             in_=x_hbm[i * P:(i + 1) * P, :]).then_inc(
                    S["sx%d" % (i % 2)], 16)
        blk.sync(sy_p0)

        VB0 = vn[0]
        vh = lambda i: VB0 + 12 * i + 6
        rn = lambda i: VB0 + 12 * i + 12

        def sc_p0(s):
            s.wait_ge(S["vself"], 2)
            def stats(i):
                s.wait_ge(S["sx%d" % (i % 2)], 16 * (i // 2 + 1))
                for half in range(2):
                    sl = x_t[i % 2][:, half * T:(half + 1) * T]
                    s.wait_ge(S["sself"], sn[0])
                    s.activation(trash_ps[:, :], sl, AF.Copy,
                                 accum_out=sab[:, 4 * i + half:4 * i + half + 1]
                                 ).then_inc(S["sself"], 1)
                    sn[0] += 1
                    s.wait_ge(S["sself"], sn[0])
                    s.activation(trash_ps[:, :], sl, AF.Square,
                                 accum_out=sab[:, 4 * i + 2 + half:4 * i + 3 + half]
                                 ).then_inc(S["sself"], 1)
                    sn[0] += 1

            def finish(i):
                s.wait_ge(S["vself"], vh(i))
                s.activation(t0c[:, 4:5], t0c[:, 3:4], AF.Sqrt).then_inc(S["ssd"], 1)
                s.wait_ge(S["vself"], rn(i))
                if i >= 1:
                    s.wait_ge(S["stp"], 4 * i)
                s.activation(xn_t[:, :], x_t[i % 2][:, :], AF.Identity,
                             bias=nbc[:, i:i + 1],
                             scale=rstdc[:, i:i + 1]).then_inc(S["sxn"], 1)

            stats(0)
            for i in range(1, NT):
                stats(i)
                finish(i - 1)
            finish(NT - 1)
        blk.scalar(sc_p0)

        def ve_p0(v):
            def VB():
                v.wait_ge(S["vself"], vn[0])

            def VC(ins):
                ins.then_inc(S["vself"], 1)
                vn[0] += 1

            def evacs(i):
                for g8 in range(4):
                    idx = 4 * i + g8
                    v.wait_ge(S["stp"], idx + 1)
                    dst = xnT[:, :].rearrange("p (cc t) -> p cc t", cc=NCC)[
                        :, 8 * g8:8 * g8 + 8, i * P:(i + 1) * P]
                    v.tensor_copy(dst, tp_ps[idx % 2][:, :].rearrange(
                        "p (k t) -> p k t", k=8)).then_inc(S["stpe"], 1)

            for i in range(NT):
                if i >= 1:
                    evacs(i - 1)
                v.wait_ge(S["sself"], 4 * (i + 1))
                VB(); VC(v.tensor_add(sumc[:, i:i + 1], sab[:, 4 * i:4 * i + 1],
                                      sab[:, 4 * i + 1:4 * i + 2]))
                VB(); VC(v.tensor_add(sqc[:, i:i + 1], sab[:, 4 * i + 2:4 * i + 3],
                                      sab[:, 4 * i + 3:4 * i + 4]))
                VB(); VC(v.tensor_scalar(meanc[:, i:i + 1], sumc[:, i:i + 1],
                                         1.0 / C, None, ALU.mult))
                VB(); VC(v.tensor_mul(t0c[:, 0:1], meanc[:, i:i + 1], meanc[:, i:i + 1]))
                VB(); VC(v.scalar_tensor_tensor(t0c[:, 1:2], sqc[:, i:i + 1],
                                                1.0 / C, t0c[:, 0:1],
                                                ALU.mult, ALU.subtract))
                VB(); VC(v.tensor_scalar(t0c[:, 3:4], t0c[:, 1:2], EPS, None, ALU.add))
                assert vn[0] == VB0 + 12 * i + 6
                v.wait_ge(S["ssd"], i + 1)
                VB(); VC(v.reciprocal(t0c[:, 5:6], t0c[:, 4:5]))
                VB(); VC(v.tensor_mul(t0c[:, 0:1], t0c[:, 5:6], t0c[:, 5:6]))
                VB(); VC(v.tensor_mul(t0c[:, 1:2], t0c[:, 0:1], t0c[:, 3:4]))
                VB(); VC(v.tensor_scalar(t0c[:, 2:3], t0c[:, 1:2], -0.5, 1.5,
                                         ALU.mult, ALU.add))
                VB(); VC(v.tensor_mul(rstdc[:, i:i + 1], t0c[:, 5:6], t0c[:, 2:3]))
                VB(); VC(v.scalar_tensor_tensor(nbc[:, i:i + 1], meanc[:, i:i + 1],
                                                -1.0, rstdc[:, i:i + 1],
                                                ALU.mult, ALU.mult))
                assert vn[0] == VB0 + 12 * i + 12
            evacs(NT - 1)
        blk.vector(ve_p0)

        def t_p0(t):
            t.wait_ge(S["sgp"], 5)
            for i in range(NT):
                t.wait_ge(S["sxn"], i + 1)
                for g8 in range(4):
                    idx = 4 * i + g8
                    if idx >= 2:
                        t.wait_ge(S["stpe"], idx - 1)
                    for k in range(8):
                        cc = 8 * g8 + k
                        mm = t.transpose(tp_ps[idx % 2][:, k * P:(k + 1) * P],
                                         xn_t[:, cc * P:(cc + 1) * P],
                                         ident_bf[:, :])
                    mm.then_inc(S["stp"], 1)
        blk.tensor(t_p0)



def _phase2(nc, blk, S, xnT, QT, wqkv_b, kt_stage, vt_dram, v_stage, gk, gv,
            bq_col, bq_q):
    with nc.sbuf_tensor("wq_sl0", [P, C], BF16) as wq_sl0, \
         nc.sbuf_tensor("wq_sl1", [P, C], BF16) as wq_sl1, \
         nc.sbuf_tensor("kv_ev0", [P, T], BF16) as kv_ev0, \
         nc.sbuf_tensor("kv_ev1", [P, T], BF16) as kv_ev1, \
         nc.sbuf_tensor("v_tok0", [P, DV], BF16) as v_tok0, \
         nc.sbuf_tensor("v_tok1", [P, DV], BF16) as v_tok1, \
         nc.psum_tensor("qkv_ps", [P, T], F32) as qkv_ps:
        wq_sl = (wq_sl0, wq_sl1)
        kv_ev = (kv_ev0, kv_ev1)
        v_tok = (v_tok0, v_tok1)
        fcs = list(range(8, 16)) + list(range(16, 24)) + list(range(0, 8))

        def g_p2(g):
            g.wait_ge(S["stp"], 64)
            for n, fc in enumerate(fcs):
                if n >= 2:
                    g.wait_ge(S["sqk"], n - 1)
                g.dma_start(out=wq_sl[n % 2][:, :].rearrange(
                    "p (cc f) -> p cc f", f=P),
                    in_=wqkv_b[fc]).then_inc(S["swq%d" % (n % 2)], 16)
            g.wait_ge(S["skst0"], 16 * 8)
            g.wait_ge(S["skst1"], 16 * 8)
            g.wait_ge(S["svst0"], 16 * 8)
            g.wait_ge(S["svst1"], 16 * 8)
            g.collective_compute(
                "AllGather", ALU.bypass, replica_groups=[list(range(8))],
                ins=[kt_stage], outs=[gk]).then_inc(S["scc"], 1)
            g.wait_ge(S["scc"], 1)
            g.collective_compute(
                "AllGather", ALU.bypass, replica_groups=[list(range(8))],
                ins=[v_stage], outs=[gv]).then_inc(S["scc"], 1)
        blk.gpsimd(g_p2)

        def t_p2(t):
            t.wait_ge(S["stpe"], 64)
            for n in range(NFC):
                t.wait_ge(S["swq%d" % (n % 2)], 16 * (n // 2 + 1))
                if n >= 1:
                    t.wait_ge(S["sqe"], n)
                for cc in range(NCC):
                    for j in range(4):
                        mm = t.matmul(qkv_ps[:, j * 512:(j + 1) * 512],
                                      wq_sl[n % 2][:, cc * P:(cc + 1) * P],
                                      xnT[:, cc * T + j * 512: cc * T + (j + 1) * 512],
                                      start=(cc == 0), stop=(cc == NCC - 1))
                mm.then_inc(S["sqk"], 1)
        blk.tensor(t_p2)

        def s_p2(s):
            s.wait_ge(S["vself"], 2)
            for n, fc in enumerate(fcs):
                s.wait_ge(S["sqk"], n + 1)
                if n < 16:
                    if n >= 2:
                        s.wait_ge(S["skst%d" % (n % 2)], 16 * (n // 2))
                    ins = s.activation(kv_ev[n % 2][:, :], qkv_ps[:, :], AF.Identity,
                                       bias=bq_col[:, fc:fc + 1])
                else:
                    dc = n - 16
                    ins = s.activation(QT[:, dc * T:(dc + 1) * T], qkv_ps[:, :],
                                       AF.Identity, bias=bq_q[:, dc:dc + 1], scale=QSCALE)
                ins.then_inc(S["sqe"], 1)
        blk.scalar(s_p2)

        def sy_p2(sy):
            for n in range(16):
                dc = n % 8
                dst = kt_stage if n < 8 else vt_dram
                sy.wait_ge(S["sqe"], n + 1)
                sy.dma_start(out=dst[dc * P:(dc + 1) * P, :],
                             in_=kv_ev[n % 2][:, :]).then_inc(S["skst%d" % (n % 2)], 16)
            for ts_ in range(NT):
                if ts_ == 0:
                    sy.wait_ge(S["skst0"], 16 * 8)
                    sy.wait_ge(S["skst1"], 16 * 8)
                if ts_ >= 2:
                    sy.wait_ge(S["svst%d" % (ts_ % 2)], 16 * (ts_ // 2))
                sy.dma_start(out=v_tok[ts_ % 2][:, :],
                             in_=vt_dram[:, ts_ * P:(ts_ + 1) * P],
                             transpose=True).then_inc(S["svtl"], 16)
                sy.wait_ge(S["svtl"], 16 * (ts_ + 1))
                sy.dma_start(out=v_stage[ts_ * P:(ts_ + 1) * P, :],
                             in_=v_tok[ts_ % 2][:, :]).then_inc(S["svst%d" % (ts_ % 2)], 16)
        blk.sync(sy_p2)


def _phase3(nc, blk, S, vn, xnT, wt_b, a_h, wr_h, bm_h,
            bf_col, ident_bf, ident_f, ffn_dram):
    g1 = [0]

    with nc.sbuf_tensor("hT_sb", [32, T], BF16) as hT_sb:

        with nc.sbuf_tensor("wrT_sb", [P, P], BF16) as wrT_sb, \
             nc.sbuf_tensor("aT_sb", [P, 1024], BF16) as aT_sb, \
             nc.sbuf_tensor("gt_sb", [4, T], F32) as gt_sb, \
             nc.sbuf_tensor("pj_sb", [32, T], F32) as pj_sb, \
             nc.sbuf_tensor("g_sb", [P, 64], F32) as g_sb, \
             nc.sbuf_tensor("ge_sb", [P, 64], F32) as ge_sb, \
             nc.sbuf_tensor("gn_sb", [P, 64], F32) as gn_sb, \
             nc.sbuf_tensor("gs_sb", [P, NT], F32) as gs_sb, \
             nc.sbuf_tensor("rgs_sb", [P, NT], F32) as rgs_sb, \
             nc.sbuf_tensor("ptok_sb", [P, 512], F32) as ptok_sb, \
             nc.sbuf_tensor("h_sb", [P, 512], BF16) as h_sb:

            with nc.sbuf_tensor("wr_sb", [4, C], BF16) as wr_sb, \
                 nc.sbuf_tensor("a_sb", [32, C], BF16) as a_sb, \
                 nc.psum_tensor("wrT_ps", [P, P], BF16) as wrT_ps, \
                 nc.psum_tensor("aT_ps", [P, 1024], BF16) as aT_ps:

                def g_p3b(g):
                    g.wait_ge(S["sqk"], NFC)
                    g.wait_ge(S["skst0"], 16 * 8)
                    g.wait_ge(S["skst1"], 16 * 8)
                    g.wait_ge(S["svst0"], 16 * 8)
                    g.wait_ge(S["svst1"], 16 * 8)
                    g.wait_ge(S["sld"], 48)
                    g.dma_start(out=wr_sb[:, :], in_=wr_h).then_inc(S["sld"], 16)
                    g.wait_ge(S["sld"], 64)
                    g.dma_start(out=a_sb[:, :], in_=a_h).then_inc(S["sld"], 16)
                blk.gpsimd(g_p3b)

                def t_p3a(t):
                    t.wait_ge(S["sld"], 16 * 5)
                    t.wait_ge(S["sqe"], NFC)
                    for cc in range(NCC):
                        t.transpose(wrT_ps[:, 4 * cc:4 * cc + 4],
                                    wr_sb[0:4, cc * P:(cc + 1) * P],
                                    ident_bf[0:4, 0:4])
                    for cc in range(NCC):
                        mm = t.transpose(aT_ps[:, 32 * cc:32 * cc + 32],
                                         a_sb[0:32, cc * P:(cc + 1) * P],
                                         ident_bf[0:32, 0:32])
                    g1[0] += 1
                    mm.then_inc(S["sg1"], 1)
                blk.tensor(t_p3a)

                def v_p3a(v):
                    v.wait_ge(S["sg1"], g1[0])
                    v.tensor_copy(wrT_sb[:, :], wrT_ps[:, :])
                    v.tensor_copy(aT_sb[:, :], aT_ps[:, :]).then_inc(S["vself"], 1)
                    vn[0] += 1
                blk.vector(v_p3a)
                M_AT = vn[0]

            with nc.psum_tensor("gt_ps", [4, T], F32) as gt_ps, \
                 nc.psum_tensor("pj_ps", [32, T], F32) as pj_ps:

                def t_p3b(t):
                    t.wait_ge(S["vself"], M_AT)
                    for cc in range(NCC):
                        for j in range(4):
                            t.matmul(gt_ps[0:4, j * 512:(j + 1) * 512],
                                     wrT_sb[:, 4 * cc:4 * cc + 4],
                                     xnT[:, cc * T + j * 512: cc * T + (j + 1) * 512],
                                     start=(cc == 0), stop=(cc == NCC - 1))
                    for cc in range(NCC):
                        for j in range(4):
                            mm = t.matmul(pj_ps[0:32, j * 512:(j + 1) * 512],
                                          aT_sb[:, 32 * cc:32 * cc + 32],
                                          xnT[:, cc * T + j * 512: cc * T + (j + 1) * 512],
                                          start=(cc == 0), stop=(cc == NCC - 1))
                    g1[0] += 1
                    mm.then_inc(S["sg1"], 1)
                blk.tensor(t_p3b)

                def v_p3b(v):
                    v.wait_ge(S["sg1"], g1[0])
                    v.tensor_copy(gt_sb[:, :], gt_ps[:, :])
                    v.tensor_copy(pj_sb[:, :], pj_ps[:, :]).then_inc(S["vself"], 1)
                    vn[0] += 1
                blk.vector(v_p3b)
                M_GP = vn[0]

            with nc.psum_tensor("gtk_ps", [P, 64], F32) as gtk_ps, \
                 nc.psum_tensor("ptk_ps", [P, 512], F32) as ptk_ps, \
                 nc.psum_tensor("hT_ps", [32, T], BF16) as hT_ps:

                def t_p3c1(t):
                    t.wait_ge(S["vself"], M_GP)
                    for ts_ in range(NT):
                        t.transpose(gtk_ps[:, 4 * ts_:4 * ts_ + 4],
                                    gt_sb[0:4, ts_ * P:(ts_ + 1) * P],
                                    ident_f[0:4, 0:4])
                    for ts_ in range(NT):
                        mm = t.transpose(ptk_ps[:, 32 * ts_:32 * ts_ + 32],
                                         pj_sb[0:32, ts_ * P:(ts_ + 1) * P],
                                         ident_f[0:32, 0:32])
                    g1[0] += 1
                    mm.then_inc(S["sg1"], 1)
                blk.tensor(t_p3c1)

                def v_p3c1(v):
                    v.wait_ge(S["sg1"], g1[0])
                    v.tensor_copy(g_sb[:, :], gtk_ps[:, :])
                    v.tensor_copy(ptok_sb[:, :], ptk_ps[:, :]).then_inc(S["vself"], 1)
                    vn[0] += 1
                blk.vector(v_p3c1)
                M_G = vn[0]

                def s_p3c(s):
                    s.wait_ge(S["vself"], M_G)
                    s.activation(ge_sb[:, :], g_sb[:, :], AF.Exp).then_inc(S["sg3"], 1)
                blk.scalar(s_p3c)

                def v_p3c2(v):
                    def VB():
                        v.wait_ge(S["vself"], vn[0])

                    def VC(ins):
                        ins.then_inc(S["vself"], 1)
                        vn[0] += 1
                    v.wait_ge(S["sg3"], 1)
                    VB(); VC(v.reduce_sum(gs_sb[:, :],
                                          ge_sb[:, :].rearrange("p (t e) -> p t e", e=4),
                                          axis=AX.X))
                    VB(); VC(v.reciprocal(rgs_sb[:, :], gs_sb[:, :]))
                    for ts_ in range(NT):
                        VB(); VC(v.tensor_scalar(gn_sb[:, 4 * ts_:4 * ts_ + 4],
                                                 ge_sb[:, 4 * ts_:4 * ts_ + 4],
                                                 rgs_sb[:, ts_:ts_ + 1], 2.0,
                                                 ALU.mult, ALU.mult))
                    for ts_ in range(NT):
                        for e in range(4):
                            o = 32 * ts_ + 8 * e
                            VB(); VC(v.tensor_scalar(h_sb[:, o:o + 8],
                                                     ptok_sb[:, o:o + 8],
                                                     gn_sb[:, 4 * ts_ + e:4 * ts_ + e + 1],
                                                     None, ALU.mult))
                blk.vector(v_p3c2)
                M_H = vn[0]

                def t_p3c2(t):
                    t.wait_ge(S["vself"], M_H)
                    for ts_ in range(NT):
                        mm = t.transpose(hT_ps[0:32, ts_ * P:(ts_ + 1) * P],
                                         h_sb[:, 32 * ts_:32 * ts_ + 32],
                                         ident_bf[:, :])
                    g1[0] += 1
                    mm.then_inc(S["sg1"], 1)
                blk.tensor(t_p3c2)

                def v_p3c3(v):
                    v.wait_ge(S["sg1"], g1[0])
                    v.tensor_copy(hT_sb[:, :], hT_ps[:, :]).then_inc(S["vself"], 1)
                    vn[0] += 1
                blk.vector(v_p3c3)
                M_HT = vn[0]

        with nc.sbuf_tensor("bmf_sb", [32, C], BF16) as bmf_sb, \
             nc.sbuf_tensor("wt_sl0", [P, C], BF16) as wt_sl0, \
             nc.sbuf_tensor("wt_sl1", [P, C], BF16) as wt_sl1, \
             nc.sbuf_tensor("ff_ev0", [P, T], F32) as ff_ev0, \
             nc.sbuf_tensor("ff_ev1", [P, T], F32) as ff_ev1, \
             nc.psum_tensor("ff_ps0", [P, T], F32) as ff_ps0, \
             nc.psum_tensor("ff_ps1", [P, T], F32) as ff_ps1:
            wt_sl = (wt_sl0, wt_sl1)
            ff_ev = (ff_ev0, ff_ev1)
            ff_ps = (ff_ps0, ff_ps1)

            def g_p3d(g):
                g.wait_ge(S["sg1"], 4)
                g.wait_ge(S["vself"], M_HT)
                g.wait_ge(S["sld"], 80)
                g.dma_start(out=bmf_sb[:, :], in_=bm_h).then_inc(S["sld"], 16)
                for m in range(NCC):
                    if m >= 2:
                        g.wait_ge(S["sfm"], m - 1)
                    g.dma_start(out=wt_sl[m % 2][:, :].rearrange(
                        "p (cc f) -> p cc f", f=P),
                        in_=wt_b[m]).then_inc(S["swt%d" % (m % 2)], 16)
            blk.gpsimd(g_p3d)

            def t_p3d(t):
                t.wait_ge(S["vself"], M_HT)
                t.wait_ge(S["sld"], 16 * 6)
                for m in range(NCC):
                    t.wait_ge(S["swt%d" % (m % 2)], 16 * (m // 2 + 1))
                    if m >= 2:
                        t.wait_ge(S["sfe"], m - 1)
                    for cc in range(NCC):
                        for j in range(4):
                            t.matmul(ff_ps[m % 2][:, j * 512:(j + 1) * 512],
                                     wt_sl[m % 2][:, cc * P:(cc + 1) * P],
                                     xnT[:, cc * T + j * 512: cc * T + (j + 1) * 512],
                                     start=(cc == 0), stop=False)
                    for j in range(4):
                        mm = t.matmul(ff_ps[m % 2][:, j * 512:(j + 1) * 512],
                                      bmf_sb[0:32, m * P:(m + 1) * P],
                                      hT_sb[0:32, j * 512:(j + 1) * 512],
                                      start=False, stop=True)
                    mm.then_inc(S["sfm"], 1)
            blk.tensor(t_p3d)

            def s_p3d(s):
                for m in range(NCC):
                    s.wait_ge(S["sfm"], m + 1)
                    if m >= 2:
                        s.wait_ge(S["sfd%d" % (m % 2)], 16 * (m // 2))
                    s.activation(ff_ev[m % 2][:, :], ff_ps[m % 2][:, :], AF.Identity,
                                 bias=bf_col[:, m:m + 1]).then_inc(S["sfe"], 1)
            blk.scalar(s_p3d)

            def sy_p3d(sy):
                for m in range(NCC):
                    sy.wait_ge(S["sfe"], m + 1)
                    sy.dma_start(out=ffn_dram[m * P:(m + 1) * P, :],
                                 in_=ff_ev[m % 2][:, :]).then_inc(S["sfd%d" % (m % 2)], 16)
            blk.sync(sy_p3d)
    return None


def _phase4(nc, blk, S, vn, QT, OT, gk, gv, ones8, ident_bf, s_sb, rs_sb):
    gk_r = gk.rearrange("(j dc p) t -> j p dc t", j=NBLK, dc=NDC, p=P)
    gv_r = gv.rearrange("(j kc p) d -> j p kc d", j=NBLK, kc=NKC, p=P)

    with nc.sbuf_tensor("kslab", [P, NDC * T], BF16) as kslab, \
         nc.sbuf_tensor("vslab", [P, NKC * DV], BF16) as vslab, \
         nc.sbuf_tensor("pbuf", [P, NKC * 512], BF16) as pbuf, \
         nc.sbuf_tensor("O_sbuf", [P, 8 * DV], F32) as O_sbuf:

        def sy_p4(sy):
            for h in range(NH):
                for j in range(NBLK):
                    idx = h * NBLK + j
                    if idx == 0:
                        sy.wait_ge(S["scc"], 2)
                        sy.wait_ge(S["sfm"], 32)
                        sy.wait_ge(S["sfd0"], 16 * 16)
                        sy.wait_ge(S["sfd1"], 16 * 16)
                    else:
                        sy.wait_ge(S["sstm"], 32 * idx)
                    sy.dma_start(out=kslab[:, :].rearrange(
                        "p (dc t) -> p dc t", dc=NDC),
                        in_=gk_r[j]).then_inc(S["skl"], 16)
                    if idx >= 1:
                        sy.wait_ge(S["spv"], 8 * idx)
                    sy.dma_start(out=vslab[:, :].rearrange(
                        "p (kc d) -> p kc d", kc=NKC),
                        in_=gv_r[j]).then_inc(S["svl"], 16)
        blk.sync(sy_p4)

        nrm_ms = {}
        for h in range(NH):
            with nc.psum_tensor("st_ps0_%d" % h, [P, 512], F32) as st_ps0, \
                 nc.psum_tensor("st_ps1_%d" % h, [P, 512], F32) as st_ps1, \
                 nc.psum_tensor("o_ps0_%d" % h, [P, DV], F32) as o_ps0, \
                 nc.psum_tensor("o_ps1_%d" % h, [P, DV], F32) as o_ps1, \
                 nc.psum_tensor("s_ps0_%d" % h, [P, 8], F32) as s_ps0, \
                 nc.psum_tensor("s_ps1_%d" % h, [P, 8], F32) as s_ps1:
                st_ps = (st_ps0, st_ps1)
                o_ps = (o_ps0, o_ps1)
                s_ps = (s_ps0, s_ps1)

                def t_p4(t, h=h, st_ps=st_ps, o_ps=o_ps, s_ps=s_ps):
                    if h == 0:
                        t.wait_ge(S["sqe"], NFC)
                        t.wait_ge(S["sfe"], 32)
                    else:
                        t.wait_ge(S["sote"], 8)
                    for j in range(NBLK):
                        idx = h * NBLK + j
                        t.wait_ge(S["skl"], 16 * (idx + 1))
                        for qg in range(2):
                            qgg = idx * 2 + qg
                            for kc in range(NKC):
                                K = qgg * NKC + kc
                                if K >= 2:
                                    t.wait_ge(S["sexp"], K - 1)
                                for dc in range(NDC):
                                    mm = t.matmul(
                                        st_ps[kc % 2][:, :],
                                        kslab[:, dc * T + kc * P: dc * T + (kc + 1) * P],
                                        QT[:, dc * T + h * 1024 + qg * 512:
                                           dc * T + h * 1024 + (qg + 1) * 512],
                                        start=(dc == 0), stop=(dc == NDC - 1))
                                mm.then_inc(S["sstm"], 1)
                            t.wait_ge(S["sexp"], (qgg + 1) * NKC)
                            if qg == 0:
                                t.wait_ge(S["svl"], 16 * (idx + 1))
                            for qs in range(4):
                                QSG = qgg * 4 + qs
                                if QSG >= 2:
                                    t.wait_ge(S["sov"], QSG - 1)
                                for kc in range(NKC):
                                    lhsT = pbuf[:, kc * 512 + qs * P:
                                                kc * 512 + (qs + 1) * P]
                                    t.matmul(o_ps[QSG % 2][:, 0:512], lhsT,
                                             vslab[:, kc * DV: kc * DV + 512],
                                             start=(kc == 0), stop=(kc == NKC - 1))
                                    t.matmul(o_ps[QSG % 2][:, 512:1024], lhsT,
                                             vslab[:, kc * DV + 512: (kc + 1) * DV],
                                             start=(kc == 0), stop=(kc == NKC - 1))
                                    mm = t.matmul(s_ps[QSG % 2][:, 0:8], lhsT,
                                                  ones8[:, :],
                                                  start=(kc == 0), stop=(kc == NKC - 1))
                                mm.then_inc(S["spv"], 1)
                blk.tensor(t_p4)

                def s_p4(s, h=h, st_ps=st_ps):
                    for j in range(NBLK):
                        idx = h * NBLK + j
                        for qg in range(2):
                            qgg = idx * 2 + qg
                            for kc in range(NKC):
                                K = qgg * NKC + kc
                                s.wait_ge(S["sstm"], K + 1)
                                if kc == 0:
                                    s.wait_ge(S["spv"], 4 * qgg)
                                    if h > 0 and j == 0 and qg == 0:
                                        s.wait_ge(S["sot"], 8 * h)
                                s.activation(pbuf[:, kc * 512:(kc + 1) * 512],
                                             st_ps[kc % 2][:, :],
                                             AF.Exp).then_inc(S["sexp"], 1)
                blk.scalar(s_p4)

                def v_p4(v, h=h, o_ps=o_ps, s_ps=s_ps):
                    for j in range(NBLK):
                        idx = h * NBLK + j
                        for qg in range(2):
                            qgg = idx * 2 + qg
                            for qs in range(4):
                                QSG = qgg * 4 + qs
                                qq = qg * 4 + qs
                                v.wait_ge(S["spv"], QSG + 1)
                                osl = O_sbuf[:, qq * DV:(qq + 1) * DV]
                                ssl = s_sb[:, h * 8 + qq: h * 8 + qq + 1]
                                if j == 0:
                                    v.tensor_copy(osl, o_ps[QSG % 2][:, :])
                                    ins = v.tensor_copy(ssl, s_ps[QSG % 2][:, 0:1])
                                else:
                                    v.tensor_tensor(osl, o_ps[QSG % 2][:, :], osl,
                                                    ALU.add)
                                    ins = v.tensor_tensor(ssl, s_ps[QSG % 2][:, 0:1],
                                                          ssl, ALU.add)
                                ins.then_inc(S["sov"], 1)
                    v.wait_ge(S["sov"], 64 * (h + 1))
                    v.wait_ge(S["vself"], vn[0])
                    v.reciprocal(rs_sb[:, h * 8:h * 8 + 8],
                                 s_sb[:, h * 8:h * 8 + 8]).then_inc(S["vself"], 1)
                    vn[0] += 1
                    for qq in range(8):
                        v.wait_ge(S["vself"], vn[0])
                        v.tensor_scalar(
                            pbuf[:, qq * 1024:(qq + 1) * 1024],
                            O_sbuf[:, qq * DV:(qq + 1) * DV],
                            rs_sb[:, h * 8 + qq: h * 8 + qq + 1], None,
                            ALU.mult).then_inc(S["vself"], 1)
                        vn[0] += 1
                        nrm_ms[h * 8 + qq] = vn[0]
                blk.vector(v_p4)

            with nc.psum_tensor("ot_ps0_%d" % h, [P, DV], BF16) as ot_ps0, \
                 nc.psum_tensor("ot_ps1_%d" % h, [P, DV], BF16) as ot_ps1:
                ot_ps = (ot_ps0, ot_ps1)

                def t_p4e(t, h=h, ot_ps=ot_ps):
                    for qq in range(8):
                        oti = h * 8 + qq
                        t.wait_ge(S["vself"], nrm_ms[oti])
                        if oti >= 2:
                            t.wait_ge(S["sote"], oti - 1)
                        for dc in range(NDC):
                            mm = t.transpose(ot_ps[oti % 2][:, dc * P:(dc + 1) * P],
                                             pbuf[:, qq * 1024 + dc * P:
                                                  qq * 1024 + (dc + 1) * P],
                                             ident_bf[:, :])
                        mm.then_inc(S["sot"], 1)
                blk.tensor(t_p4e)

                def v_p4e(v, h=h, ot_ps=ot_ps):
                    for qq in range(8):
                        oti = h * 8 + qq
                        v.wait_ge(S["sot"], oti + 1)
                        dst = OT[:, :].rearrange("p (dc q) -> p dc q", dc=NDC)[
                            :, :, h * 1024 + qq * P: h * 1024 + (qq + 1) * P]
                        v.tensor_copy(dst, ot_ps[oti % 2][:, :].rearrange(
                            "p (dc q) -> p dc q", dc=NDC)).then_inc(S["sote"], 1)
                blk.vector(v_p4e)


def _phase5(nc, blk, S, OT, wo_b, ffn_dram, x_hbm, y_hbm, bo_col, ident_f):
    with nc.sbuf_tensor("yT_grp", [P, 8 * T], F32) as yT_grp, \
         nc.sbuf_tensor("wo_sl0", [P, DV], BF16) as wo_sl0, \
         nc.sbuf_tensor("wo_sl1", [P, DV], BF16) as wo_sl1, \
         nc.sbuf_tensor("ffc0", [P, T], F32) as ffc0, \
         nc.sbuf_tensor("ffc1", [P, T], F32) as ffc1, \
         nc.sbuf_tensor("x2_0", [P, 1024], F32) as x2_0, \
         nc.sbuf_tensor("x2_1", [P, 1024], F32) as x2_1, \
         nc.sbuf_tensor("yr0", [P, 1024], F32) as yr0, \
         nc.sbuf_tensor("yr1", [P, 1024], F32) as yr1, \
         nc.psum_tensor("a_ps0", [P, 1024], F32) as a_ps0, \
         nc.psum_tensor("a_ps1", [P, 1024], F32) as a_ps1, \
         nc.psum_tensor("ytp_ps0", [P, 1024], F32) as ytp_ps0, \
         nc.psum_tensor("ytp_ps1", [P, 1024], F32) as ytp_ps1:
        wo_sl = (wo_sl0, wo_sl1)
        ffc = (ffc0, ffc1)
        x2 = (x2_0, x2_1)
        yr = (yr0, yr1)
        a_ps = (a_ps0, a_ps1)
        ytp_ps = (ytp_ps0, ytp_ps1)

        def g_p5(g):
            for cm in range(NCC):
                if cm == 0:
                    g.wait_ge(S["sote"], 16)
                if cm >= 2:
                    g.wait_ge(S["sat"], 2 * (cm - 1))
                g.dma_start(out=wo_sl[cm % 2][:, :].rearrange(
                    "p (dc f) -> p dc f", f=P),
                    in_=wo_b[cm]).then_inc(S["swo%d" % (cm % 2)], 16)
        blk.gpsimd(g_p5)

        def sc_p5(s):
            for G in range(4):
                for ts_ in range(NT):
                    yi = G * NT + ts_
                    if yi == 0:
                        s.wait_ge(S["sote"], 16)
                    if yi >= 2:
                        s.wait_ge(S["syv"], yi - 1)
                    s.dma_start(out=x2[ts_ % 2][:, :],
                                in_=x_hbm[ts_ * P:(ts_ + 1) * P,
                                          G * 1024:(G + 1) * 1024]).then_inc(
                        S["sx2%d" % (yi % 2)], 16)
        blk.scalar(sc_p5)

        def t_p5(t):
            for G in range(4):
                for m in range(8):
                    cm = G * 8 + m
                    t.wait_ge(S["swo%d" % (cm % 2)], 16 * (cm // 2 + 1))
                    for th in range(2):
                        at = cm * 2 + th
                        if at >= 2:
                            t.wait_ge(S["sy1"], at - 1)
                        for dc in range(NDC):
                            for j in range(2):
                                mm = t.matmul(
                                    a_ps[at % 2][:, j * 512:(j + 1) * 512],
                                    wo_sl[cm % 2][:, dc * P:(dc + 1) * P],
                                    OT[:, dc * T + th * 1024 + j * 512:
                                       dc * T + th * 1024 + (j + 1) * 512],
                                    start=(dc == 0), stop=(dc == NDC - 1))
                        mm.then_inc(S["sat"], 1)
                t.wait_ge(S["sy1"], 16 * (G + 1))
                for ts_ in range(NT):
                    yi = G * NT + ts_
                    if yi >= 2:
                        t.wait_ge(S["syv"], yi - 1)
                    for m in range(8):
                        mm = t.transpose(ytp_ps[ts_ % 2][:, m * P:(m + 1) * P],
                                         yT_grp[:, m * T + ts_ * P:
                                                m * T + (ts_ + 1) * P],
                                         ident_f[:, :])
                    mm.then_inc(S["syt"], 1)
        blk.tensor(t_p5)

        def v_p5(v):
            for G in range(4):
                for m in range(8):
                    cm = G * 8 + m
                    for th in range(2):
                        at = cm * 2 + th
                        v.wait_ge(S["sat"], at + 1)
                        v.wait_ge(S["sfr%d" % (cm % 2)], 16 * (cm // 2 + 1))
                        if G >= 1 and m == 0 and th == 0:
                            v.wait_ge(S["syt"], 16 * G)
                        v.scalar_tensor_tensor(
                            yT_grp[:, m * T + th * 1024: m * T + (th + 1) * 1024],
                            a_ps[at % 2][:, :], bo_col[:, cm:cm + 1],
                            ffc[cm % 2][:, th * 1024:(th + 1) * 1024],
                            ALU.add, ALU.add).then_inc(S["sy1"], 1)
                for ts_ in range(NT):
                    yi = G * NT + ts_
                    v.wait_ge(S["syt"], yi + 1)
                    v.wait_ge(S["sx2%d" % (yi % 2)], 16 * (yi // 2 + 1))
                    if yi >= 2:
                        v.wait_ge(S["syo%d" % (yi % 2)], 16 * (yi // 2))
                    v.tensor_tensor(yr[ts_ % 2][:, :], ytp_ps[ts_ % 2][:, :],
                                    x2[ts_ % 2][:, :], ALU.add).then_inc(S["syv"], 1)
        blk.vector(v_p5)

        def sy_p5(sy):
            for G in range(4):
                for m in range(8):
                    cm = G * 8 + m
                    if cm == 0:
                        sy.wait_ge(S["sote"], 16)
                    sy.wait_ge(S["sfd%d" % (cm % 2)], 16 * (cm // 2 + 1))
                    if cm >= 2:
                        sy.wait_ge(S["sy1"], 2 * (cm - 1))
                    sy.dma_start(out=ffc[cm % 2][:, :],
                                 in_=ffn_dram[cm * P:(cm + 1) * P, :]).then_inc(
                        S["sfr%d" % (cm % 2)], 16)
                for ts_ in range(NT):
                    yi = G * NT + ts_
                    sy.wait_ge(S["syv"], yi + 1)
                    sy.dma_start(out=y_hbm[ts_ * P:(ts_ + 1) * P,
                                           G * 1024:(G + 1) * 1024],
                                 in_=yr[ts_ % 2][:, :]).then_inc(S["syo%d" % (yi % 2)], 16)
            sy.wait_ge(S["syo0"], 16 * 32)
            sy.wait_ge(S["syo1"], 16 * 32)
        blk.sync(sy_p5)


def _emit_taps(nc, blk, S, vn, tap_out, upto, bufs):
    if not tap_out:
        return

    def g_tap(g):
        waits = {"t_xnT": ("stpe", 64), "t_QT": ("sqe", NFC), "t_gk": ("scc", 2),
                 "t_gv": ("scc", 2), "t_ffn": ("sfd0", 16 * 16),
                 "t_OT": ("sote", 16), "t_s": ("sote", 16)}
        srcs = {"t_xnT": lambda: bufs["xnT"][:, :], "t_QT": lambda: bufs["QT"][:, :],
                "t_gk": lambda: bufs["gk"], "t_gv": lambda: bufs["gv"],
                "t_ffn": lambda: bufs["ffn_dram"], "t_OT": lambda: bufs["OT"][:, :],
                "t_s": lambda: bufs["s_sb"][:, :]}
        n = 0
        for name in tap_out:
            sem, val = waits[name]
            g.wait_ge(S[sem], val)
            if name == "t_ffn":
                g.wait_ge(S["sfd1"], 16 * 16)
            g.dma_start(out=tap_out[name], in_=srcs[name]()).then_inc(S["stap"], 16)
            n += 1
            g.wait_ge(S["stap"], 16 * n)
    blk.gpsimd(g_tap)


def _prep(L0_attn_Wqkv, L0_attn_Wo, L0_ffn_W, L0_ffn_A, L0_ffn_Bm, L0_ffn_Wr,
          L0_attn_bqkv, L0_attn_bo, L0_ffn_b):
    f32 = np.float32
    wqkv = np.ascontiguousarray(
        np.asarray(L0_attn_Wqkv, f32).reshape(32, 128, 24, 128).transpose(2, 1, 0, 3))
    wo = np.ascontiguousarray(
        np.asarray(L0_attn_Wo, f32).reshape(8, 128, 32, 128).transpose(2, 1, 0, 3))
    wt = np.ascontiguousarray(
        np.asarray(L0_ffn_W, f32).T.reshape(32, 128, 32, 128).transpose(2, 1, 0, 3))
    a_nat = np.ascontiguousarray(np.asarray(L0_ffn_A, f32).reshape(32, 4096))
    wr_nat = np.ascontiguousarray(np.asarray(L0_ffn_Wr, f32))
    bmf = np.ascontiguousarray(
        np.asarray(L0_ffn_Bm, f32).transpose(0, 2, 1).reshape(32, 4096))
    return {
        "wqkv": wqkv, "wo": wo, "wt": wt, "a_nat": a_nat, "wr_nat": wr_nat,
        "bmf": bmf,
        "bqkv": np.ascontiguousarray(np.asarray(L0_attn_bqkv, f32)[None, :]),
        "bo": np.ascontiguousarray(np.asarray(L0_attn_bo, f32)[None, :]),
        "bffn": np.ascontiguousarray(np.asarray(L0_ffn_b, f32)[None, :]),
    }


def kernel(L0_attn_x, L0_attn_Wqkv, L0_attn_bqkv, L0_attn_Wo, L0_attn_bo,
           L0_ffn_W, L0_ffn_b, L0_ffn_A, L0_ffn_Bm, L0_ffn_Wr):
    x = np.ascontiguousarray(np.asarray(L0_attn_x, np.float32).reshape(SEQ, C))
    shared = _prep(L0_attn_Wqkv, L0_attn_Wo, L0_ffn_W, L0_ffn_A, L0_ffn_Bm,
                   L0_ffn_Wr, L0_attn_bqkv, L0_attn_bo, L0_ffn_b)
    nc, _ = build_nc(upto=5)
    in_maps = []
    for c in range(8):
        m = dict(shared)
        m["x"] = np.ascontiguousarray(x[c * T:(c + 1) * T])
        in_maps.append(m)
    res = run_bass_kernel_spmd(nc, in_maps, core_ids=list(range(8)))
    y = np.concatenate([r["y"] for r in res.results], axis=0)
    return y.reshape(1, SEQ, C).astype(np.float32)


# version 02
# speedup vs baseline: 1.0657x; 1.0657x over previous
import numpy as np
from contextlib import ExitStack

import concourse.bass as bass
import concourse.mybir as mybir
from concourse.bass_utils import run_bass_kernel_spmd

F32 = mybir.dt.float32
BF16 = mybir.dt.bfloat16
AF = mybir.ActivationFunctionType
ALU = mybir.AluOpType
AX = mybir.AxisListType

P = 128
T = 2048
NT = 16
C = 4096
NCC = 32
DV = 1024
NDC = 8
NFC = 24
SEQ = 16384
NBLK = 8
NKC = 16
NH = 2
EPS = 1e-5
QSCALE = 1.0 / 32.0


def build_nc(upto=5, taps=()):
    nc = bass.Bass("TRN2", target_bir_lowering=False)

    x_hbm = nc.dram_tensor("x", [T, C], F32, kind="ExternalInput").ap()
    wqkv_b = nc.dram_tensor("wqkv", [NFC, P, NCC, P], F32, kind="ExternalInput").ap()
    wo_b = nc.dram_tensor("wo", [NCC, P, NDC, P], F32, kind="ExternalInput").ap()
    wt_b = nc.dram_tensor("wt", [NCC, P, NCC, P], F32, kind="ExternalInput").ap()
    bqkv_h = nc.dram_tensor("bqkv", [1, 3072], F32, kind="ExternalInput").ap()
    bo_h = nc.dram_tensor("bo", [1, C], F32, kind="ExternalInput").ap()
    bffn_h = nc.dram_tensor("bffn", [1, C], F32, kind="ExternalInput").ap()
    a_h = nc.dram_tensor("a_nat", [32, C], F32, kind="ExternalInput").ap()
    wr_h = nc.dram_tensor("wr_nat", [4, C], F32, kind="ExternalInput").ap()
    bm_h = nc.dram_tensor("bmf", [32, C], F32, kind="ExternalInput").ap()
    y_hbm = nc.dram_tensor("y", [T, C], F32, kind="ExternalOutput").ap()

    kt_stage = nc.dram_tensor("kt_stage", [DV, T], BF16).ap()
    vt_dram = nc.dram_tensor("vt_dram", [DV, T], BF16).ap()
    v_stage = nc.dram_tensor("v_stage", [T, DV], BF16).ap()
    gk = nc.dram_tensor("gk", [8 * DV, T], BF16).ap()
    gv = nc.dram_tensor("gv", [SEQ, DV], BF16).ap()
    ffn_dram = nc.dram_tensor("ffn_dram", [C, T], F32).ap()

    tap_out = {n: nc.dram_tensor(n, sh, dt, kind="ExternalOutput").ap()
               for n, sh, dt in taps}

    ident_bf = nc.alloc_sbuf_tensor("ident_bf", [P, P], BF16)
    ident_f = nc.alloc_sbuf_tensor("ident_f", [P, P], F32)
    ones8 = nc.alloc_sbuf_tensor("ones8", [P, 8], BF16)
    bq_col = nc.alloc_sbuf_tensor("bq_col", [P, NFC], F32)
    bq_q = nc.alloc_sbuf_tensor("bq_q", [P, 8], F32)
    bo_col = nc.alloc_sbuf_tensor("bo_col", [P, NCC], F32)
    bf_col = nc.alloc_sbuf_tensor("bf_col", [P, NCC], F32)
    sumc = nc.alloc_sbuf_tensor("sumc", [P, NT], F32)
    sqc = nc.alloc_sbuf_tensor("sqc", [P, NT], F32)
    meanc = nc.alloc_sbuf_tensor("meanc", [P, NT], F32)
    rstdc = nc.alloc_sbuf_tensor("rstdc", [P, NT], F32)
    nbc = nc.alloc_sbuf_tensor("nbc", [P, NT], F32)
    t0c = nc.alloc_sbuf_tensor("t0c", [P, 6], F32)
    sab = nc.alloc_sbuf_tensor("sab", [P, 8 * NT], F32)
    s_sb = nc.alloc_sbuf_tensor("s_sb", [P, NT], F32)
    rs_sb = nc.alloc_sbuf_tensor("rs_sb", [P, NT], F32)

    st = ExitStack()
    with st:
        E = st.enter_context
        names = ("sgp sld sbt sx0 sx1 ssd sxn stp stpe "
                 "swq0 swq1 sqk sqe skst0 skst1 svtl svst0 svst1 scc "
                 "sg1 sg3 swt0 swt1 sfm sfe sfd0 sfd1 "
                 "skl svl sstm sexp spv sov sot sote sccv "
                 "swo0 swo1 sfr0 sfr1 sat sy1 syt sx20 sx21 syv "
                 "syo0 syo1 vself sself stap").split()
        S = {n: E(nc.semaphore(n)) for n in names}
        vn = [0]
        sn = [0]

        blk = E(nc.Block())

        def g_init(g):
            g.memset(ident_bf[:, :], 0.0).then_inc(S["sgp"], 1)
            g.wait_ge(S["sgp"], 1)
            g.affine_select(out=ident_bf[:, :], in_=ident_bf[:, :],
                            compare_op=ALU.not_equal, fill=1.0, base=0,
                            pattern=[[-1, P]], channel_multiplier=1).then_inc(S["sgp"], 1)
            g.memset(ident_f[:, :], 0.0).then_inc(S["sgp"], 1)
            g.wait_ge(S["sgp"], 3)
            g.affine_select(out=ident_f[:, :], in_=ident_f[:, :],
                            compare_op=ALU.not_equal, fill=1.0, base=0,
                            pattern=[[-1, P]], channel_multiplier=1).then_inc(S["sgp"], 1)
            g.memset(ones8[:, :], 1.0).then_inc(S["sgp"], 1)
        blk.gpsimd(g_init)

        with nc.sbuf_tensor("bq_row", [1, 3072], F32) as bq_row, \
             nc.sbuf_tensor("bo_row", [1, C], F32) as bo_row, \
             nc.sbuf_tensor("bf_row", [1, C], F32) as bf_row, \
             nc.psum_tensor("bias_ps", [P, 128], F32) as bias_ps:

            def g_rows(g):
                g.dma_start(out=bq_row[:, :], in_=bqkv_h).then_inc(S["sld"], 16)
                g.wait_ge(S["sld"], 16)
                g.dma_start(out=bo_row[:, :], in_=bo_h).then_inc(S["sld"], 16)
                g.wait_ge(S["sld"], 32)
                g.dma_start(out=bf_row[:, :], in_=bffn_h).then_inc(S["sld"], 16)
            blk.gpsimd(g_rows)

            def t_bias(t):
                t.wait_ge(S["sgp"], 5)
                t.wait_ge(S["sld"], 48)
                k = 0
                for row, cnt in ((bq_row, NFC), (bo_row, NCC), (bf_row, NCC)):
                    for j in range(cnt):
                        mm = t.transpose(bias_ps[:, k:k + 1],
                                         row[0:1, j * P:(j + 1) * P],
                                         ident_f[0:1, 0:1])
                        k += 1
                mm.then_inc(S["sbt"], 1)
            blk.tensor(t_bias)

            def v_bias(v):
                v.wait_ge(S["sbt"], 1)
                v.tensor_copy(bq_col[:, :], bias_ps[:, 0:NFC]).then_inc(S["vself"], 1)
                vn[0] += 1
                v.tensor_copy(bo_col[:, :], bias_ps[:, NFC:NFC + NCC])
                v.tensor_copy(bf_col[:, :], bias_ps[:, NFC + NCC:NFC + 2 * NCC])
                v.wait_ge(S["vself"], vn[0])
                v.tensor_scalar(bq_q[:, :], bq_col[:, 0:8], QSCALE, None,
                                ALU.mult).then_inc(S["vself"], 1)
                vn[0] += 1
            blk.vector(v_bias)
        VBIAS_DONE = vn[0]

        QT_cm = nc.sbuf_tensor("QT", [P, NDC * T], BF16)
        QT = QT_cm.__enter__()
        xnT_cm = nc.sbuf_tensor("xnT", [P, NCC * T], BF16)
        xnT = xnT_cm.__enter__()

        _phase0(nc, blk, S, vn, sn, x_hbm, xnT, ident_bf,
                sumc, sqc, meanc, rstdc, nbc, t0c, sab)

        if upto >= 2:
            _phase2(nc, blk, S, xnT, QT, wqkv_b, kt_stage, vt_dram, v_stage,
                    gk, gv, bq_col, bq_q)
        if upto >= 3:
            m_hT = _phase3(nc, blk, S, vn, xnT, wt_b, a_h, wr_h, bm_h,
                           bf_col, ident_bf, ident_f, ffn_dram)
        xnT_cm.__exit__(None, None, None)

        OT_cm = nc.sbuf_tensor("OT", [P, NDC * T], BF16)
        OT = OT_cm.__enter__()
        if upto >= 4:
            _phase4(nc, blk, S, vn, QT, OT, gk, gv, ones8, ident_bf, s_sb, rs_sb)
        if upto >= 5:
            _phase5(nc, blk, S, OT, wo_b, ffn_dram, x_hbm, y_hbm, bo_col, ident_f)

        _emit_taps(nc, blk, S, vn, tap_out, upto,
                   dict(xnT=xnT if upto < 4 else None, QT=QT, OT=OT,
                        gk=gk, gv=gv, ffn_dram=ffn_dram,
                        s_sb=s_sb, rs_sb=rs_sb))
        OT_cm.__exit__(None, None, None)
        QT_cm.__exit__(None, None, None)

    return nc, tap_out


def _phase0(nc, blk, S, vn, sn, x_hbm, xnT, ident_bf,
            sumc, sqc, meanc, rstdc, nbc, t0c, sab):
    with nc.sbuf_tensor("x_t0", [P, C], F32) as x_t0, \
         nc.sbuf_tensor("x_t1", [P, C], F32) as x_t1, \
         nc.sbuf_tensor("xn_t", [P, C], BF16) as xn_t, \
         nc.sbuf_tensor("vtrash", [P, 1024], F32) as vtrash, \
         nc.psum_tensor("trash_ps", [P, T], F32) as trash_ps, \
         nc.psum_tensor("tp_ps0", [P, 1024], BF16) as tp_ps0, \
         nc.psum_tensor("tp_ps1", [P, 1024], BF16) as tp_ps1:
        x_t = (x_t0, x_t1)
        tp_ps = (tp_ps0, tp_ps1)

        def sy_p0(sy):
            sy.wait_ge(S["sbt"], 1)
            for i in range(NT):
                if i >= 2:
                    sy.wait_ge(S["sxn"], i - 1)
                sy.dma_start(out=x_t[i % 2][:, :],
                             in_=x_hbm[i * P:(i + 1) * P, :]).then_inc(
                    S["sx%d" % (i % 2)], 16)
        blk.sync(sy_p0)

        VB0 = vn[0]
        vh = lambda i: VB0 + 18 * i + 12
        rn = lambda i: VB0 + 18 * i + 18

        def sc_p0(s):
            s.wait_ge(S["vself"], 2)
            def stats(i):
                s.wait_ge(S["sx%d" % (i % 2)], 16 * (i // 2 + 1))
                for half in range(2):
                    sl = x_t[i % 2][:, half * T:(half + 1) * T]
                    s.wait_ge(S["sself"], sn[0])
                    s.activation(trash_ps[:, :], sl, AF.Square,
                                 accum_out=sab[:, 8 * i + 4 + half:8 * i + 5 + half]
                                 ).then_inc(S["sself"], 1)
                    sn[0] += 1

            def finish(i):
                s.wait_ge(S["vself"], vh(i))
                s.activation(t0c[:, 4:5], t0c[:, 3:4], AF.Sqrt).then_inc(S["ssd"], 1)
                s.wait_ge(S["vself"], rn(i))
                if i >= 1:
                    s.wait_ge(S["stp"], 4 * i)
                s.activation(xn_t[:, :], x_t[i % 2][:, :], AF.Identity,
                             bias=nbc[:, i:i + 1],
                             scale=rstdc[:, i:i + 1]).then_inc(S["sxn"], 1)

            stats(0)
            for i in range(1, NT):
                stats(i)
                finish(i - 1)
            finish(NT - 1)
        blk.scalar(sc_p0)

        def ve_p0(v):
            def VB():
                v.wait_ge(S["vself"], vn[0])

            def VC(ins):
                ins.then_inc(S["vself"], 1)
                vn[0] += 1

            def evacs(i):
                for g8 in range(4):
                    idx = 4 * i + g8
                    v.wait_ge(S["stp"], idx + 1)
                    dst = xnT[:, :].rearrange("p (cc t) -> p cc t", cc=NCC)[
                        :, 8 * g8:8 * g8 + 8, i * P:(i + 1) * P]
                    v.tensor_copy(dst, tp_ps[idx % 2][:, :].rearrange(
                        "p (k t) -> p k t", k=8)).then_inc(S["stpe"], 1)

            for i in range(NT):
                if i >= 1:
                    evacs(i - 1)
                v.wait_ge(S["sx%d" % (i % 2)], 16 * (i // 2 + 1))
                for q in range(4):
                    VB(); VC(v.tensor_scalar(
                        vtrash[:, :], x_t[i % 2][:, q * 1024:(q + 1) * 1024],
                        1.0, None, ALU.mult, ALU.add,
                        accum_out=sab[:, 8 * i + q:8 * i + q + 1]))
                VB(); VC(v.tensor_add(t0c[:, 0:1], sab[:, 8 * i:8 * i + 1],
                                      sab[:, 8 * i + 1:8 * i + 2]))
                VB(); VC(v.tensor_add(t0c[:, 1:2], sab[:, 8 * i + 2:8 * i + 3],
                                      sab[:, 8 * i + 3:8 * i + 4]))
                VB(); VC(v.tensor_add(sumc[:, i:i + 1], t0c[:, 0:1], t0c[:, 1:2]))
                VB(); VC(v.tensor_scalar(meanc[:, i:i + 1], sumc[:, i:i + 1],
                                         1.0 / C, None, ALU.mult))
                VB(); VC(v.tensor_mul(t0c[:, 0:1], meanc[:, i:i + 1], meanc[:, i:i + 1]))
                v.wait_ge(S["sself"], 2 * (i + 1))
                VB(); VC(v.tensor_add(sqc[:, i:i + 1], sab[:, 8 * i + 4:8 * i + 5],
                                      sab[:, 8 * i + 5:8 * i + 6]))
                VB(); VC(v.scalar_tensor_tensor(t0c[:, 1:2], sqc[:, i:i + 1],
                                                1.0 / C, t0c[:, 0:1],
                                                ALU.mult, ALU.subtract))
                VB(); VC(v.tensor_scalar(t0c[:, 3:4], t0c[:, 1:2], EPS, None, ALU.add))
                assert vn[0] == VB0 + 18 * i + 12
                v.wait_ge(S["ssd"], i + 1)
                VB(); VC(v.reciprocal(t0c[:, 5:6], t0c[:, 4:5]))
                VB(); VC(v.tensor_mul(t0c[:, 0:1], t0c[:, 5:6], t0c[:, 5:6]))
                VB(); VC(v.tensor_mul(t0c[:, 1:2], t0c[:, 0:1], t0c[:, 3:4]))
                VB(); VC(v.tensor_scalar(t0c[:, 2:3], t0c[:, 1:2], -0.5, 1.5,
                                         ALU.mult, ALU.add))
                VB(); VC(v.tensor_mul(rstdc[:, i:i + 1], t0c[:, 5:6], t0c[:, 2:3]))
                VB(); VC(v.scalar_tensor_tensor(nbc[:, i:i + 1], meanc[:, i:i + 1],
                                                -1.0, rstdc[:, i:i + 1],
                                                ALU.mult, ALU.mult))
                assert vn[0] == VB0 + 18 * i + 18
            evacs(NT - 1)
        blk.vector(ve_p0)

        def t_p0(t):
            t.wait_ge(S["sgp"], 5)
            for i in range(NT):
                t.wait_ge(S["sxn"], i + 1)
                for g8 in range(4):
                    idx = 4 * i + g8
                    if idx >= 2:
                        t.wait_ge(S["stpe"], idx - 1)
                    for k in range(8):
                        cc = 8 * g8 + k
                        mm = t.transpose(tp_ps[idx % 2][:, k * P:(k + 1) * P],
                                         xn_t[:, cc * P:(cc + 1) * P],
                                         ident_bf[:, :])
                    mm.then_inc(S["stp"], 1)
        blk.tensor(t_p0)



def _phase2(nc, blk, S, xnT, QT, wqkv_b, kt_stage, vt_dram, v_stage, gk, gv,
            bq_col, bq_q):
    with nc.sbuf_tensor("wq_sl0", [P, C], BF16) as wq_sl0, \
         nc.sbuf_tensor("wq_sl1", [P, C], BF16) as wq_sl1, \
         nc.sbuf_tensor("kv_ev0", [P, T], BF16) as kv_ev0, \
         nc.sbuf_tensor("kv_ev1", [P, T], BF16) as kv_ev1, \
         nc.sbuf_tensor("v_tok0", [P, DV], BF16) as v_tok0, \
         nc.sbuf_tensor("v_tok1", [P, DV], BF16) as v_tok1, \
         nc.psum_tensor("qkv_ps0", [P, T], F32) as qkv_ps0, \
         nc.psum_tensor("qkv_ps1", [P, T], F32) as qkv_ps1:
        qkv_ps = (qkv_ps0, qkv_ps1)
        wq_sl = (wq_sl0, wq_sl1)
        kv_ev = (kv_ev0, kv_ev1)
        v_tok = (v_tok0, v_tok1)
        fcs = list(range(8, 16)) + list(range(16, 24)) + list(range(0, 8))

        def g_p2(g):
            g.wait_ge(S["stp"], 64)
            for n, fc in enumerate(fcs):
                if n >= 2:
                    g.wait_ge(S["sqk"], n - 1)
                g.dma_start(out=wq_sl[n % 2][:, :].rearrange(
                    "p (cc f) -> p cc f", f=P),
                    in_=wqkv_b[fc]).then_inc(S["swq%d" % (n % 2)], 16)
            g.wait_ge(S["skst0"], 16 * 8)
            g.wait_ge(S["skst1"], 16 * 8)
            g.wait_ge(S["svst0"], 16 * 8)
            g.wait_ge(S["svst1"], 16 * 8)
            g.collective_compute(
                "AllGather", ALU.bypass, replica_groups=[list(range(8))],
                ins=[kt_stage], outs=[gk]).then_inc(S["scc"], 1)
            g.collective_compute(
                "AllGather", ALU.bypass, replica_groups=[list(range(8))],
                ins=[v_stage], outs=[gv]).then_inc(S["sccv"], 1)
        blk.gpsimd(g_p2)

        def t_p2(t):
            t.wait_ge(S["stpe"], 64)
            for n in range(NFC):
                t.wait_ge(S["swq%d" % (n % 2)], 16 * (n // 2 + 1))
                if n >= 2:
                    t.wait_ge(S["sqe"], n - 1)
                for cc in range(NCC):
                    for j in range(4):
                        mm = t.matmul(qkv_ps[n % 2][:, j * 512:(j + 1) * 512],
                                      wq_sl[n % 2][:, cc * P:(cc + 1) * P],
                                      xnT[:, cc * T + j * 512: cc * T + (j + 1) * 512],
                                      start=(cc == 0), stop=(cc == NCC - 1))
                mm.then_inc(S["sqk"], 1)
        blk.tensor(t_p2)

        def s_p2(s):
            s.wait_ge(S["vself"], 2)
            for n, fc in enumerate(fcs):
                s.wait_ge(S["sqk"], n + 1)
                if n < 16:
                    if n >= 2:
                        s.wait_ge(S["skst%d" % (n % 2)], 16 * (n // 2))
                    ins = s.activation(kv_ev[n % 2][:, :], qkv_ps[n % 2][:, :],
                                       AF.Identity, bias=bq_col[:, fc:fc + 1])
                else:
                    dc = n - 16
                    ins = s.activation(QT[:, dc * T:(dc + 1) * T], qkv_ps[n % 2][:, :],
                                       AF.Identity, bias=bq_q[:, dc:dc + 1], scale=QSCALE)
                ins.then_inc(S["sqe"], 1)
        blk.scalar(s_p2)

        def sy_p2(sy):
            for n in range(16):
                dc = n % 8
                dst = kt_stage if n < 8 else vt_dram
                sy.wait_ge(S["sqe"], n + 1)
                sy.dma_start(out=dst[dc * P:(dc + 1) * P, :],
                             in_=kv_ev[n % 2][:, :]).then_inc(S["skst%d" % (n % 2)], 16)
            for ts_ in range(NT):
                if ts_ == 0:
                    sy.wait_ge(S["skst0"], 16 * 8)
                    sy.wait_ge(S["skst1"], 16 * 8)
                if ts_ >= 2:
                    sy.wait_ge(S["svst%d" % (ts_ % 2)], 16 * (ts_ // 2))
                sy.dma_start(out=v_tok[ts_ % 2][:, :],
                             in_=vt_dram[:, ts_ * P:(ts_ + 1) * P],
                             transpose=True).then_inc(S["svtl"], 16)
                sy.wait_ge(S["svtl"], 16 * (ts_ + 1))
                sy.dma_start(out=v_stage[ts_ * P:(ts_ + 1) * P, :],
                             in_=v_tok[ts_ % 2][:, :]).then_inc(S["svst%d" % (ts_ % 2)], 16)
        blk.sync(sy_p2)


def _phase3(nc, blk, S, vn, xnT, wt_b, a_h, wr_h, bm_h,
            bf_col, ident_bf, ident_f, ffn_dram):
    g1 = [0]

    with nc.sbuf_tensor("hT_sb", [32, T], BF16) as hT_sb:

        with nc.sbuf_tensor("wrT_sb", [P, P], BF16) as wrT_sb, \
             nc.sbuf_tensor("aT_sb", [P, 1024], BF16) as aT_sb, \
             nc.sbuf_tensor("gt_sb", [4, T], F32) as gt_sb, \
             nc.sbuf_tensor("pj_sb", [32, T], F32) as pj_sb, \
             nc.sbuf_tensor("g_sb", [P, 64], F32) as g_sb, \
             nc.sbuf_tensor("ge_sb", [P, 64], F32) as ge_sb, \
             nc.sbuf_tensor("gn_sb", [P, 64], F32) as gn_sb, \
             nc.sbuf_tensor("gs_sb", [P, NT], F32) as gs_sb, \
             nc.sbuf_tensor("rgs_sb", [P, NT], F32) as rgs_sb, \
             nc.sbuf_tensor("ptok_sb", [P, 512], F32) as ptok_sb, \
             nc.sbuf_tensor("h_sb", [P, 512], BF16) as h_sb:

            with nc.sbuf_tensor("wr_sb", [4, C], BF16) as wr_sb, \
                 nc.sbuf_tensor("a_sb", [32, C], BF16) as a_sb, \
                 nc.psum_tensor("wrT_ps", [P, P], BF16) as wrT_ps, \
                 nc.psum_tensor("aT_ps", [P, 1024], BF16) as aT_ps:

                def g_p3b(g):
                    g.wait_ge(S["sqk"], NFC)
                    g.wait_ge(S["skst0"], 16 * 8)
                    g.wait_ge(S["skst1"], 16 * 8)
                    g.wait_ge(S["svst0"], 16 * 8)
                    g.wait_ge(S["svst1"], 16 * 8)
                    g.wait_ge(S["sld"], 48)
                    g.dma_start(out=wr_sb[:, :], in_=wr_h).then_inc(S["sld"], 16)
                    g.wait_ge(S["sld"], 64)
                    g.dma_start(out=a_sb[:, :], in_=a_h).then_inc(S["sld"], 16)
                blk.gpsimd(g_p3b)

                def t_p3a(t):
                    t.wait_ge(S["sld"], 16 * 5)
                    t.wait_ge(S["sqe"], NFC)
                    for cc in range(NCC):
                        t.transpose(wrT_ps[:, 4 * cc:4 * cc + 4],
                                    wr_sb[0:4, cc * P:(cc + 1) * P],
                                    ident_bf[0:4, 0:4])
                    for cc in range(NCC):
                        mm = t.transpose(aT_ps[:, 32 * cc:32 * cc + 32],
                                         a_sb[0:32, cc * P:(cc + 1) * P],
                                         ident_bf[0:32, 0:32])
                    g1[0] += 1
                    mm.then_inc(S["sg1"], 1)
                blk.tensor(t_p3a)

                def v_p3a(v):
                    v.wait_ge(S["sg1"], g1[0])
                    v.tensor_copy(wrT_sb[:, :], wrT_ps[:, :])
                    v.tensor_copy(aT_sb[:, :], aT_ps[:, :]).then_inc(S["vself"], 1)
                    vn[0] += 1
                blk.vector(v_p3a)
                M_AT = vn[0]

            with nc.psum_tensor("gt_ps", [4, T], F32) as gt_ps, \
                 nc.psum_tensor("pj_ps", [32, T], F32) as pj_ps:

                def t_p3b(t):
                    t.wait_ge(S["vself"], M_AT)
                    for cc in range(NCC):
                        for j in range(4):
                            t.matmul(gt_ps[0:4, j * 512:(j + 1) * 512],
                                     wrT_sb[:, 4 * cc:4 * cc + 4],
                                     xnT[:, cc * T + j * 512: cc * T + (j + 1) * 512],
                                     start=(cc == 0), stop=(cc == NCC - 1))
                    for cc in range(NCC):
                        for j in range(4):
                            mm = t.matmul(pj_ps[0:32, j * 512:(j + 1) * 512],
                                          aT_sb[:, 32 * cc:32 * cc + 32],
                                          xnT[:, cc * T + j * 512: cc * T + (j + 1) * 512],
                                          start=(cc == 0), stop=(cc == NCC - 1))
                    g1[0] += 1
                    mm.then_inc(S["sg1"], 1)
                blk.tensor(t_p3b)

                def v_p3b(v):
                    v.wait_ge(S["sg1"], g1[0])
                    v.tensor_copy(gt_sb[:, :], gt_ps[:, :])
                    v.tensor_copy(pj_sb[:, :], pj_ps[:, :]).then_inc(S["vself"], 1)
                    vn[0] += 1
                blk.vector(v_p3b)
                M_GP = vn[0]

            with nc.psum_tensor("gtk_ps", [P, 64], F32) as gtk_ps, \
                 nc.psum_tensor("ptk_ps", [P, 512], F32) as ptk_ps, \
                 nc.psum_tensor("hT_ps", [32, T], BF16) as hT_ps:

                def t_p3c1(t):
                    t.wait_ge(S["vself"], M_GP)
                    for ts_ in range(NT):
                        t.transpose(gtk_ps[:, 4 * ts_:4 * ts_ + 4],
                                    gt_sb[0:4, ts_ * P:(ts_ + 1) * P],
                                    ident_f[0:4, 0:4])
                    for ts_ in range(NT):
                        mm = t.transpose(ptk_ps[:, 32 * ts_:32 * ts_ + 32],
                                         pj_sb[0:32, ts_ * P:(ts_ + 1) * P],
                                         ident_f[0:32, 0:32])
                    g1[0] += 1
                    mm.then_inc(S["sg1"], 1)
                blk.tensor(t_p3c1)

                def v_p3c1(v):
                    v.wait_ge(S["sg1"], g1[0])
                    v.tensor_copy(g_sb[:, :], gtk_ps[:, :])
                    v.tensor_copy(ptok_sb[:, :], ptk_ps[:, :]).then_inc(S["vself"], 1)
                    vn[0] += 1
                blk.vector(v_p3c1)
                M_G = vn[0]

                def s_p3c(s):
                    s.wait_ge(S["vself"], M_G)
                    s.activation(ge_sb[:, :], g_sb[:, :], AF.Exp).then_inc(S["sg3"], 1)
                blk.scalar(s_p3c)

                def v_p3c2(v):
                    def VB():
                        v.wait_ge(S["vself"], vn[0])

                    def VC(ins):
                        ins.then_inc(S["vself"], 1)
                        vn[0] += 1
                    v.wait_ge(S["sg3"], 1)
                    VB(); VC(v.reduce_sum(gs_sb[:, :],
                                          ge_sb[:, :].rearrange("p (t e) -> p t e", e=4),
                                          axis=AX.X))
                    VB(); VC(v.reciprocal(rgs_sb[:, :], gs_sb[:, :]))
                    VB()
                    for ts_ in range(NT):
                        VC(v.tensor_scalar(gn_sb[:, 4 * ts_:4 * ts_ + 4],
                                           ge_sb[:, 4 * ts_:4 * ts_ + 4],
                                           rgs_sb[:, ts_:ts_ + 1], 2.0,
                                           ALU.mult, ALU.mult))
                    VB()
                    for ts_ in range(NT):
                        for e in range(4):
                            o = 32 * ts_ + 8 * e
                            VC(v.tensor_scalar(h_sb[:, o:o + 8],
                                               ptok_sb[:, o:o + 8],
                                               gn_sb[:, 4 * ts_ + e:4 * ts_ + e + 1],
                                               None, ALU.mult))
                blk.vector(v_p3c2)
                M_H = vn[0]

                def t_p3c2(t):
                    t.wait_ge(S["vself"], M_H)
                    for ts_ in range(NT):
                        mm = t.transpose(hT_ps[0:32, ts_ * P:(ts_ + 1) * P],
                                         h_sb[:, 32 * ts_:32 * ts_ + 32],
                                         ident_bf[:, :])
                    g1[0] += 1
                    mm.then_inc(S["sg1"], 1)
                blk.tensor(t_p3c2)

                def v_p3c3(v):
                    v.wait_ge(S["sg1"], g1[0])
                    v.tensor_copy(hT_sb[:, :], hT_ps[:, :]).then_inc(S["vself"], 1)
                    vn[0] += 1
                blk.vector(v_p3c3)
                M_HT = vn[0]

        with nc.sbuf_tensor("bmf_sb", [32, C], BF16) as bmf_sb, \
             nc.sbuf_tensor("wt_sl0", [P, C], BF16) as wt_sl0, \
             nc.sbuf_tensor("wt_sl1", [P, C], BF16) as wt_sl1, \
             nc.sbuf_tensor("ff_ev0", [P, T], F32) as ff_ev0, \
             nc.sbuf_tensor("ff_ev1", [P, T], F32) as ff_ev1, \
             nc.psum_tensor("ff_ps0", [P, T], F32) as ff_ps0, \
             nc.psum_tensor("ff_ps1", [P, T], F32) as ff_ps1:
            wt_sl = (wt_sl0, wt_sl1)
            ff_ev = (ff_ev0, ff_ev1)
            ff_ps = (ff_ps0, ff_ps1)

            def g_p3d(g):
                g.wait_ge(S["sg1"], 4)
                g.wait_ge(S["vself"], M_HT)
                g.wait_ge(S["sld"], 80)
                g.dma_start(out=bmf_sb[:, :], in_=bm_h).then_inc(S["sld"], 16)
                for m in range(NCC):
                    if m >= 2:
                        g.wait_ge(S["sfm"], m - 1)
                    g.dma_start(out=wt_sl[m % 2][:, :].rearrange(
                        "p (cc f) -> p cc f", f=P),
                        in_=wt_b[m]).then_inc(S["swt%d" % (m % 2)], 16)
            blk.gpsimd(g_p3d)

            def t_p3d(t):
                t.wait_ge(S["vself"], M_HT)
                t.wait_ge(S["sld"], 16 * 6)
                for m in range(NCC):
                    t.wait_ge(S["swt%d" % (m % 2)], 16 * (m // 2 + 1))
                    if m >= 2:
                        t.wait_ge(S["sfe"], m - 1)
                    for cc in range(NCC):
                        for j in range(4):
                            t.matmul(ff_ps[m % 2][:, j * 512:(j + 1) * 512],
                                     wt_sl[m % 2][:, cc * P:(cc + 1) * P],
                                     xnT[:, cc * T + j * 512: cc * T + (j + 1) * 512],
                                     start=(cc == 0), stop=False)
                    for j in range(4):
                        mm = t.matmul(ff_ps[m % 2][:, j * 512:(j + 1) * 512],
                                      bmf_sb[0:32, m * P:(m + 1) * P],
                                      hT_sb[0:32, j * 512:(j + 1) * 512],
                                      start=False, stop=True)
                    mm.then_inc(S["sfm"], 1)
            blk.tensor(t_p3d)

            def s_p3d(s):
                for m in range(NCC):
                    s.wait_ge(S["sfm"], m + 1)
                    if m >= 2:
                        s.wait_ge(S["sfd%d" % (m % 2)], 16 * (m // 2))
                    s.activation(ff_ev[m % 2][:, :], ff_ps[m % 2][:, :], AF.Identity,
                                 bias=bf_col[:, m:m + 1]).then_inc(S["sfe"], 1)
            blk.scalar(s_p3d)

            def sy_p3d(sy):
                for m in range(NCC):
                    sy.wait_ge(S["sfe"], m + 1)
                    sy.dma_start(out=ffn_dram[m * P:(m + 1) * P, :],
                                 in_=ff_ev[m % 2][:, :]).then_inc(S["sfd%d" % (m % 2)], 16)
            blk.sync(sy_p3d)
    return None


def _phase4(nc, blk, S, vn, QT, OT, gk, gv, ones8, ident_bf, s_sb, rs_sb):
    gk_r = gk.rearrange("(j dc p) t -> j p dc t", j=NBLK, dc=NDC, p=P)
    gv_r = gv.rearrange("(j kc p) d -> j p kc d", j=NBLK, kc=NKC, p=P)

    with nc.sbuf_tensor("kslab", [P, NDC * T], BF16) as kslab, \
         nc.sbuf_tensor("vslab", [P, NKC * DV], BF16) as vslab, \
         nc.sbuf_tensor("pbuf", [P, NKC * 512], BF16) as pbuf, \
         nc.sbuf_tensor("O_sbuf", [P, 8 * DV], F32) as O_sbuf:

        def sy_p4(sy):
            for h in range(NH):
                for j in range(NBLK):
                    idx = h * NBLK + j
                    if idx == 0:
                        sy.wait_ge(S["scc"], 1)
                        sy.wait_ge(S["sccv"], 1)
                        sy.wait_ge(S["sfm"], 32)
                        sy.wait_ge(S["sfd0"], 16 * 16)
                        sy.wait_ge(S["sfd1"], 16 * 16)
                    else:
                        sy.wait_ge(S["sstm"], 32 * idx)
                    sy.dma_start(out=kslab[:, :].rearrange(
                        "p (dc t) -> p dc t", dc=NDC),
                        in_=gk_r[j]).then_inc(S["skl"], 16)
                    if idx >= 1:
                        sy.wait_ge(S["spv"], 8 * idx)
                    sy.dma_start(out=vslab[:, :].rearrange(
                        "p (kc d) -> p kc d", kc=NKC),
                        in_=gv_r[j]).then_inc(S["svl"], 16)
        blk.sync(sy_p4)

        nrm_ms = {}
        for h in range(NH):
            with nc.psum_tensor("st_ps0_%d" % h, [P, 512], F32) as st_ps0, \
                 nc.psum_tensor("st_ps1_%d" % h, [P, 512], F32) as st_ps1, \
                 nc.psum_tensor("o_ps0_%d" % h, [P, DV], F32) as o_ps0, \
                 nc.psum_tensor("o_ps1_%d" % h, [P, DV], F32) as o_ps1, \
                 nc.psum_tensor("s_ps0_%d" % h, [P, 8], F32) as s_ps0, \
                 nc.psum_tensor("s_ps1_%d" % h, [P, 8], F32) as s_ps1:
                st_ps = (st_ps0, st_ps1)
                o_ps = (o_ps0, o_ps1)
                s_ps = (s_ps0, s_ps1)

                def t_p4(t, h=h, st_ps=st_ps, o_ps=o_ps, s_ps=s_ps):
                    if h == 0:
                        t.wait_ge(S["sqe"], NFC)
                        t.wait_ge(S["sfe"], 32)
                    else:
                        t.wait_ge(S["sote"], 8)
                    for j in range(NBLK):
                        idx = h * NBLK + j
                        t.wait_ge(S["skl"], 16 * (idx + 1))
                        for qg in range(2):
                            qgg = idx * 2 + qg
                            for kc in range(NKC):
                                K = qgg * NKC + kc
                                if K >= 2:
                                    t.wait_ge(S["sexp"], K - 1)
                                for dc in range(NDC):
                                    mm = t.matmul(
                                        st_ps[kc % 2][:, :],
                                        kslab[:, dc * T + kc * P: dc * T + (kc + 1) * P],
                                        QT[:, dc * T + h * 1024 + qg * 512:
                                           dc * T + h * 1024 + (qg + 1) * 512],
                                        start=(dc == 0), stop=(dc == NDC - 1))
                                mm.then_inc(S["sstm"], 1)
                            t.wait_ge(S["sexp"], (qgg + 1) * NKC)
                            if qg == 0:
                                t.wait_ge(S["svl"], 16 * (idx + 1))
                            for qs in range(4):
                                QSG = qgg * 4 + qs
                                if QSG >= 2:
                                    t.wait_ge(S["sov"], QSG - 1)
                                for kc in range(NKC):
                                    lhsT = pbuf[:, kc * 512 + qs * P:
                                                kc * 512 + (qs + 1) * P]
                                    t.matmul(o_ps[QSG % 2][:, 0:512], lhsT,
                                             vslab[:, kc * DV: kc * DV + 512],
                                             start=(kc == 0), stop=(kc == NKC - 1))
                                    t.matmul(o_ps[QSG % 2][:, 512:1024], lhsT,
                                             vslab[:, kc * DV + 512: (kc + 1) * DV],
                                             start=(kc == 0), stop=(kc == NKC - 1))
                                    mm = t.matmul(s_ps[QSG % 2][:, 0:8], lhsT,
                                                  ones8[:, :],
                                                  start=(kc == 0), stop=(kc == NKC - 1))
                                mm.then_inc(S["spv"], 1)
                blk.tensor(t_p4)

                def s_p4(s, h=h, st_ps=st_ps):
                    for j in range(NBLK):
                        idx = h * NBLK + j
                        for qg in range(2):
                            qgg = idx * 2 + qg
                            for kc in range(NKC):
                                K = qgg * NKC + kc
                                s.wait_ge(S["sstm"], K + 1)
                                if kc == 0:
                                    s.wait_ge(S["spv"], 4 * qgg)
                                    if h > 0 and j == 0 and qg == 0:
                                        s.wait_ge(S["sot"], 8 * h)
                                s.activation(pbuf[:, kc * 512:(kc + 1) * 512],
                                             st_ps[kc % 2][:, :],
                                             AF.Exp).then_inc(S["sexp"], 1)
                blk.scalar(s_p4)

                def v_p4(v, h=h, o_ps=o_ps, s_ps=s_ps):
                    for j in range(NBLK):
                        idx = h * NBLK + j
                        for qg in range(2):
                            qgg = idx * 2 + qg
                            for qs in range(4):
                                QSG = qgg * 4 + qs
                                qq = qg * 4 + qs
                                v.wait_ge(S["spv"], QSG + 1)
                                osl = O_sbuf[:, qq * DV:(qq + 1) * DV]
                                ssl = s_sb[:, h * 8 + qq: h * 8 + qq + 1]
                                if j == 0:
                                    v.tensor_copy(osl, o_ps[QSG % 2][:, :])
                                    ins = v.tensor_copy(ssl, s_ps[QSG % 2][:, 0:1])
                                else:
                                    v.tensor_tensor(osl, o_ps[QSG % 2][:, :], osl,
                                                    ALU.add)
                                    ins = v.tensor_tensor(ssl, s_ps[QSG % 2][:, 0:1],
                                                          ssl, ALU.add)
                                ins.then_inc(S["sov"], 1)
                    v.wait_ge(S["sov"], 64 * (h + 1))
                    v.wait_ge(S["vself"], vn[0])
                    v.reciprocal(rs_sb[:, h * 8:h * 8 + 8],
                                 s_sb[:, h * 8:h * 8 + 8]).then_inc(S["vself"], 1)
                    vn[0] += 1
                    v.wait_ge(S["vself"], vn[0])
                    for qq in range(8):
                        v.tensor_scalar(
                            pbuf[:, qq * 1024:(qq + 1) * 1024],
                            O_sbuf[:, qq * DV:(qq + 1) * DV],
                            rs_sb[:, h * 8 + qq: h * 8 + qq + 1], None,
                            ALU.mult).then_inc(S["vself"], 1)
                        vn[0] += 1
                        nrm_ms[h * 8 + qq] = vn[0]
                blk.vector(v_p4)

            with nc.psum_tensor("ot_ps0_%d" % h, [P, DV], BF16) as ot_ps0, \
                 nc.psum_tensor("ot_ps1_%d" % h, [P, DV], BF16) as ot_ps1:
                ot_ps = (ot_ps0, ot_ps1)

                def t_p4e(t, h=h, ot_ps=ot_ps):
                    for qq in range(8):
                        oti = h * 8 + qq
                        t.wait_ge(S["vself"], nrm_ms[oti])
                        if oti >= 2:
                            t.wait_ge(S["sote"], oti - 1)
                        for dc in range(NDC):
                            mm = t.transpose(ot_ps[oti % 2][:, dc * P:(dc + 1) * P],
                                             pbuf[:, qq * 1024 + dc * P:
                                                  qq * 1024 + (dc + 1) * P],
                                             ident_bf[:, :])
                        mm.then_inc(S["sot"], 1)
                blk.tensor(t_p4e)

                def v_p4e(v, h=h, ot_ps=ot_ps):
                    for qq in range(8):
                        oti = h * 8 + qq
                        v.wait_ge(S["sot"], oti + 1)
                        dst = OT[:, :].rearrange("p (dc q) -> p dc q", dc=NDC)[
                            :, :, h * 1024 + qq * P: h * 1024 + (qq + 1) * P]
                        v.tensor_copy(dst, ot_ps[oti % 2][:, :].rearrange(
                            "p (dc q) -> p dc q", dc=NDC)).then_inc(S["sote"], 1)
                blk.vector(v_p4e)


def _phase5(nc, blk, S, OT, wo_b, ffn_dram, x_hbm, y_hbm, bo_col, ident_f):
    with nc.sbuf_tensor("yT_grp", [P, 8 * T], F32) as yT_grp, \
         nc.sbuf_tensor("wo_sl0", [P, DV], BF16) as wo_sl0, \
         nc.sbuf_tensor("wo_sl1", [P, DV], BF16) as wo_sl1, \
         nc.sbuf_tensor("ffc0", [P, T], F32) as ffc0, \
         nc.sbuf_tensor("ffc1", [P, T], F32) as ffc1, \
         nc.sbuf_tensor("x2_0", [P, 1024], F32) as x2_0, \
         nc.sbuf_tensor("x2_1", [P, 1024], F32) as x2_1, \
         nc.sbuf_tensor("yr0", [P, 1024], F32) as yr0, \
         nc.sbuf_tensor("yr1", [P, 1024], F32) as yr1, \
         nc.psum_tensor("a_ps0", [P, 1024], F32) as a_ps0, \
         nc.psum_tensor("a_ps1", [P, 1024], F32) as a_ps1, \
         nc.psum_tensor("ytp_ps0", [P, 1024], F32) as ytp_ps0, \
         nc.psum_tensor("ytp_ps1", [P, 1024], F32) as ytp_ps1:
        wo_sl = (wo_sl0, wo_sl1)
        ffc = (ffc0, ffc1)
        x2 = (x2_0, x2_1)
        yr = (yr0, yr1)
        a_ps = (a_ps0, a_ps1)
        ytp_ps = (ytp_ps0, ytp_ps1)

        def g_p5(g):
            for cm in range(NCC):
                if cm == 0:
                    g.wait_ge(S["sote"], 16)
                if cm >= 2:
                    g.wait_ge(S["sat"], 2 * (cm - 1))
                g.dma_start(out=wo_sl[cm % 2][:, :].rearrange(
                    "p (dc f) -> p dc f", f=P),
                    in_=wo_b[cm]).then_inc(S["swo%d" % (cm % 2)], 16)
        blk.gpsimd(g_p5)

        def sc_p5(s):
            for G in range(4):
                for ts_ in range(NT):
                    yi = G * NT + ts_
                    if yi == 0:
                        s.wait_ge(S["sote"], 16)
                    if yi >= 2:
                        s.wait_ge(S["syv"], yi - 1)
                    s.dma_start(out=x2[ts_ % 2][:, :],
                                in_=x_hbm[ts_ * P:(ts_ + 1) * P,
                                          G * 1024:(G + 1) * 1024]).then_inc(
                        S["sx2%d" % (yi % 2)], 16)
        blk.scalar(sc_p5)

        def t_p5(t):
            for G in range(4):
                for m in range(8):
                    cm = G * 8 + m
                    t.wait_ge(S["swo%d" % (cm % 2)], 16 * (cm // 2 + 1))
                    for th in range(2):
                        at = cm * 2 + th
                        if at >= 2:
                            t.wait_ge(S["sy1"], at - 1)
                        for dc in range(NDC):
                            for j in range(2):
                                mm = t.matmul(
                                    a_ps[at % 2][:, j * 512:(j + 1) * 512],
                                    wo_sl[cm % 2][:, dc * P:(dc + 1) * P],
                                    OT[:, dc * T + th * 1024 + j * 512:
                                       dc * T + th * 1024 + (j + 1) * 512],
                                    start=(dc == 0), stop=(dc == NDC - 1))
                        mm.then_inc(S["sat"], 1)
                t.wait_ge(S["sy1"], 16 * (G + 1))
                for ts_ in range(NT):
                    yi = G * NT + ts_
                    if yi >= 2:
                        t.wait_ge(S["syv"], yi - 1)
                    for m in range(8):
                        mm = t.transpose(ytp_ps[ts_ % 2][:, m * P:(m + 1) * P],
                                         yT_grp[:, m * T + ts_ * P:
                                                m * T + (ts_ + 1) * P],
                                         ident_f[:, :])
                    mm.then_inc(S["syt"], 1)
        blk.tensor(t_p5)

        def v_p5(v):
            for G in range(4):
                for m in range(8):
                    cm = G * 8 + m
                    for th in range(2):
                        at = cm * 2 + th
                        v.wait_ge(S["sat"], at + 1)
                        v.wait_ge(S["sfr%d" % (cm % 2)], 16 * (cm // 2 + 1))
                        if G >= 1 and m == 0 and th == 0:
                            v.wait_ge(S["syt"], 16 * G)
                        v.scalar_tensor_tensor(
                            yT_grp[:, m * T + th * 1024: m * T + (th + 1) * 1024],
                            a_ps[at % 2][:, :], bo_col[:, cm:cm + 1],
                            ffc[cm % 2][:, th * 1024:(th + 1) * 1024],
                            ALU.add, ALU.add).then_inc(S["sy1"], 1)
                for ts_ in range(NT):
                    yi = G * NT + ts_
                    v.wait_ge(S["syt"], yi + 1)
                    v.wait_ge(S["sx2%d" % (yi % 2)], 16 * (yi // 2 + 1))
                    if yi >= 2:
                        v.wait_ge(S["syo%d" % (yi % 2)], 16 * (yi // 2))
                    v.tensor_tensor(yr[ts_ % 2][:, :], ytp_ps[ts_ % 2][:, :],
                                    x2[ts_ % 2][:, :], ALU.add).then_inc(S["syv"], 1)
        blk.vector(v_p5)

        def sy_p5(sy):
            for G in range(4):
                for m in range(8):
                    cm = G * 8 + m
                    if cm == 0:
                        sy.wait_ge(S["sote"], 16)
                    sy.wait_ge(S["sfd%d" % (cm % 2)], 16 * (cm // 2 + 1))
                    if cm >= 2:
                        sy.wait_ge(S["sy1"], 2 * (cm - 1))
                    sy.dma_start(out=ffc[cm % 2][:, :],
                                 in_=ffn_dram[cm * P:(cm + 1) * P, :]).then_inc(
                        S["sfr%d" % (cm % 2)], 16)
                for ts_ in range(NT):
                    yi = G * NT + ts_
                    sy.wait_ge(S["syv"], yi + 1)
                    sy.dma_start(out=y_hbm[ts_ * P:(ts_ + 1) * P,
                                           G * 1024:(G + 1) * 1024],
                                 in_=yr[ts_ % 2][:, :]).then_inc(S["syo%d" % (yi % 2)], 16)
            sy.wait_ge(S["syo0"], 16 * 32)
            sy.wait_ge(S["syo1"], 16 * 32)
        blk.sync(sy_p5)


def _emit_taps(nc, blk, S, vn, tap_out, upto, bufs):
    if not tap_out:
        return

    def g_tap(g):
        waits = {"t_xnT": ("stpe", 64), "t_QT": ("sqe", NFC), "t_gk": ("scc", 1),
                 "t_gv": ("sccv", 1), "t_ffn": ("sfd0", 16 * 16),
                 "t_OT": ("sote", 16), "t_s": ("sote", 16)}
        srcs = {"t_xnT": lambda: bufs["xnT"][:, :], "t_QT": lambda: bufs["QT"][:, :],
                "t_gk": lambda: bufs["gk"], "t_gv": lambda: bufs["gv"],
                "t_ffn": lambda: bufs["ffn_dram"], "t_OT": lambda: bufs["OT"][:, :],
                "t_s": lambda: bufs["s_sb"][:, :]}
        n = 0
        for name in tap_out:
            sem, val = waits[name]
            g.wait_ge(S[sem], val)
            if name == "t_ffn":
                g.wait_ge(S["sfd1"], 16 * 16)
            g.dma_start(out=tap_out[name], in_=srcs[name]()).then_inc(S["stap"], 16)
            n += 1
            g.wait_ge(S["stap"], 16 * n)
    blk.gpsimd(g_tap)


def _prep(L0_attn_Wqkv, L0_attn_Wo, L0_ffn_W, L0_ffn_A, L0_ffn_Bm, L0_ffn_Wr,
          L0_attn_bqkv, L0_attn_bo, L0_ffn_b):
    f32 = np.float32
    wqkv = np.ascontiguousarray(
        np.asarray(L0_attn_Wqkv, f32).reshape(32, 128, 24, 128).transpose(2, 1, 0, 3))
    wo = np.ascontiguousarray(
        np.asarray(L0_attn_Wo, f32).reshape(8, 128, 32, 128).transpose(2, 1, 0, 3))
    wt = np.ascontiguousarray(
        np.asarray(L0_ffn_W, f32).T.reshape(32, 128, 32, 128).transpose(2, 1, 0, 3))
    a_nat = np.ascontiguousarray(np.asarray(L0_ffn_A, f32).reshape(32, 4096))
    wr_nat = np.ascontiguousarray(np.asarray(L0_ffn_Wr, f32))
    bmf = np.ascontiguousarray(
        np.asarray(L0_ffn_Bm, f32).transpose(0, 2, 1).reshape(32, 4096))
    return {
        "wqkv": wqkv, "wo": wo, "wt": wt, "a_nat": a_nat, "wr_nat": wr_nat,
        "bmf": bmf,
        "bqkv": np.ascontiguousarray(np.asarray(L0_attn_bqkv, f32)[None, :]),
        "bo": np.ascontiguousarray(np.asarray(L0_attn_bo, f32)[None, :]),
        "bffn": np.ascontiguousarray(np.asarray(L0_ffn_b, f32)[None, :]),
    }


def kernel(L0_attn_x, L0_attn_Wqkv, L0_attn_bqkv, L0_attn_Wo, L0_attn_bo,
           L0_ffn_W, L0_ffn_b, L0_ffn_A, L0_ffn_Bm, L0_ffn_Wr):
    x = np.ascontiguousarray(np.asarray(L0_attn_x, np.float32).reshape(SEQ, C))
    shared = _prep(L0_attn_Wqkv, L0_attn_Wo, L0_ffn_W, L0_ffn_A, L0_ffn_Bm,
                   L0_ffn_Wr, L0_attn_bqkv, L0_attn_bo, L0_ffn_b)
    nc, _ = build_nc(upto=5)
    in_maps = []
    for c in range(8):
        m = dict(shared)
        m["x"] = np.ascontiguousarray(x[c * T:(c + 1) * T])
        in_maps.append(m)
    res = run_bass_kernel_spmd(nc, in_maps, core_ids=list(range(8)))
    y = np.concatenate([r["y"] for r in res.results], axis=0)
    return y.reshape(1, SEQ, C).astype(np.float32)
